# version 02
# baseline (speedup 1.0000x reference)
import os
os.environ.setdefault("TRN_ALLOW_ICI_ALL_TO_ALL", "1")

import numpy as np
import ml_dtypes
from contextlib import ExitStack
import concourse.bass as bass
import concourse.mybir as mybir
from concourse.bass_utils import run_bass_kernel_spmd

F32 = mybir.dt.float32
BF16 = mybir.dt.bfloat16
I32 = mybir.dt.int32
F16 = mybir.dt.float16
U8 = mybir.dt.uint8
AF = mybir.ActivationFunctionType
ALU = mybir.AluOpType
AX = mybir.AxisListType
bfdt = ml_dtypes.bfloat16

DTSIZE = {F32: 4, BF16: 2, I32: 4, F16: 2, U8: 1}

W = 8
T = 8192
TC = T // W
D = 2048
NDT = D // 128
HPC = 2
DH = 128
KC = 512
NKC = T // KC
NQT = T // 128
QSC = 1.0 / float(np.sqrt(128.0))
MASKV = -66504.0 * QSC
EPS_LN = 1e-5
EPS_RMS = 1e-6
I0 = 5504
I1 = 5632
E4 = 4096
EC = 512
NET = 4
NS = 16
RDT = 128


class Sem:
    def __init__(self, st, nc, name):
        self.h = st.enter_context(nc.semaphore(name))
        self.n = 0
        self.base = 0


class Arena:
    def __init__(self, st, nc, name, nbytes):
        self.t = st.enter_context(nc.sbuf_tensor(name, [128, nbytes], U8))
        self.nbytes = nbytes
        self.off = 0
        self.marks = []
        self.hi = 0

    def alloc(self, n_elems, dt):
        sz = n_elems * DTSIZE[dt]
        self.off = (self.off + 63) & ~63
        assert self.off + sz <= self.nbytes, f"arena overflow: {self.off + sz} > {self.nbytes}"
        v = self.t[:, self.off:self.off + sz].bitcast(dt)
        self.off += sz
        self.hi = max(self.hi, self.off)
        return v

    def mark(self):
        self.marks.append(self.off)

    def release(self):
        self.off = self.marks.pop()


class Builder:
    def __init__(self, phases, taps=(), detect_races=True):
        self.phases = set(phases)
        self.taps = set(taps)
        self.nc = bass.Bass("TRN2", target_bir_lowering=False,
                            detect_race_conditions=detect_races)
        self.q = {"sync": [], "vector": [], "scalar": [], "tensor": [], "gpsimd": []}
        self.st = ExitStack()
        self.dram = {}
        self.outputs = []

    def on(self, eng, fn):
        self.q[eng].append(fn)

    def wait(self, eng, sem, val=None):
        v = sem.n if val is None else val
        if v > 0:
            self.on(eng, lambda e, h=sem.h, v=v: e.wait_ge(h, v))

    def waitr(self, eng, sem, rel):
        v = sem.base + rel
        if v > 0 and rel > 0:
            self.on(eng, lambda e, h=sem.h, v=v: e.wait_ge(h, v))

    def sem(self, name):
        return Sem(self.st, self.nc, name)

    def pool_reset(self):
        self._pool_i = 0

    def psem(self):
        if not hasattr(self, "_pool"):
            self._pool = []
            self._pool_i = 0
        if self._pool_i == len(self._pool):
            self._pool.append(Sem(self.st, self.nc, f"pool{self._pool_i}"))
        s = self._pool[self._pool_i]
        s.base = s.n
        self._pool_i += 1
        return s

    def psem2(self):
        return [self.psem(), self.psem()]

    def din(self, name, shape, dt):
        ap = self.nc.dram_tensor(name, list(shape), dt, kind="ExternalInput").ap()
        self.dram[name] = ap
        return ap

    def dint(self, name, shape, dt):
        ap = self.nc.dram_tensor(name, list(shape), dt).ap()
        self.dram[name] = ap
        return ap

    def dout(self, name, shape, dt):
        ap = self.nc.dram_tensor(name, list(shape), dt, kind="ExternalOutput").ap()
        self.dram[name] = ap
        self.outputs.append(name)
        return ap

    def dma(self, eng, sem, out, in_):
        sem.n += 16
        self.on(eng, lambda e, o=out, i=in_, h=sem.h: e.dma_start(out=o, in_=i).then_inc(h, 16))
        return sem.n

    def sem2(self, name):
        return [Sem(self.st, self.nc, name + "a"), Sem(self.st, self.nc, name + "b")]

    def collective(self, kind, op, in_ap, out_ap, sem):
        sem.n += 1
        def f(e, k=kind, o=op, i=in_ap, u=out_ap, h=sem.h):
            e.collective_compute(k, o, replica_groups=[list(range(W))],
                                 ins=[i.opt()], outs=[u.opt()]).then_inc(h, 1)
        self.on("gpsimd", f)

    def tap(self, name, hbm_ap, rows, cols, dt, gate_sem=None):
        if name not in self.taps:
            return
        o = self.dout("tap_" + name, [rows, cols], dt)
        s = self.sem("sTap_" + name)
        if gate_sem is not None:
            self.wait("sync", gate_sem)
        r = 0
        while r < rows:
            pr = min(128, rows - r)
            self.dma("sync", s, o[r:r + pr, :], hbm_ap[r:r + pr, :])
            r += pr
        self.wait("sync", s)

    def finish(self):
        nc = self.nc
        with nc.Block() as block:
            @block.sync
            def _(e):
                for f in self.q["sync"]:
                    f(e)

            @block.vector
            def _(e):
                for f in self.q["vector"]:
                    f(e)

            @block.scalar
            def _(e):
                for f in self.q["scalar"]:
                    f(e)

            @block.tensor
            def _(e):
                for f in self.q["tensor"]:
                    f(e)

            @block.gpsimd
            def _(e):
                for f in self.q["gpsimd"]:
                    f(e)
        self.st.close()
        return nc


def mm(b, psum, lhsT, rhs, start, stop, inc=None):
    def f(e, p=psum, l=lhsT, r=rhs, s0=start, s1=stop, i=inc):
        ins = e.matmul(p, l, r, start=s0, stop=s1)
        if i is not None:
            ins.then_inc(i[0], i[1])
    b.on("tensor", f)


def build(phases=("p1",), taps=(), detect_races=True):
    b = Builder(phases, taps, detect_races)
    nc = b.nc
    st = b.st

    xtf = b.din("xtf", [D, TC], F32)
    onesb = b.din("onesb", [128, 1], BF16)
    eyeb = b.din("eyeb", [128, 128], BF16)
    ones1f = b.din("ones1f", [1, 128], F32)
    consts = b.din("consts", [128, 8], F32)

    arena = Arena(st, nc, "arena", 204 * 1024)
    ps = [st.enter_context(nc.psum_tensor(f"ps{i}", [128, 512], F32)) for i in range(8)]

    sCst = b.sem("sCst")
    sGP = b.sem("sGP")

    sb_ones = arena.alloc(1, BF16)
    sb_eye = arena.alloc(128, BF16)
    sb_ones1f = arena.alloc(128, F32)
    sb_consts = arena.alloc(8, F32)
    b.dma("sync", sCst, sb_ones, onesb[:, :])
    b.dma("sync", sCst, sb_eye, eyeb[:, :])
    b.dma("sync", sCst, sb_ones1f[0:1, :], ones1f[:, :])
    b.dma("sync", sCst, sb_consts, consts[:, :])

    C = dict(arena=arena, ps=ps, sCst=sCst, sGP=sGP, sb_ones=sb_ones,
             sb_eye=sb_eye, sb_ones1f=sb_ones1f, sb_consts=sb_consts, xtf=xtf)

    sAGxt = b.sem("sAGxt")
    ag_xt_in = b.dint("ag_xt_in", [D, TC], BF16)
    XTG = b.dint("XTG", [W * D, TC], BF16)

    arena.mark()
    xt_f32 = arena.alloc(NDT * TC, F32)
    xt_bf = arena.alloc(NDT * TC, BF16)
    sCast = b.sem("sCast")
    sP0L = b.sem("sP0L")
    sP0S = b.sem("sP0S")
    for t in range(NDT):
        b.dma("sync", sP0L, xt_f32[:, t * TC:(t + 1) * TC], xtf[t * 128:(t + 1) * 128, :])
    b.wait("vector", sP0L)
    sCast.n += 1
    b.on("vector", lambda e, o=xt_bf, i=xt_f32, h=sCast.h: e.tensor_copy(out=o, in_=i).then_inc(h, 1))
    b.wait("sync", sCast)
    for t in range(NDT):
        b.dma("sync", sP0S, ag_xt_in[t * 128:(t + 1) * 128, :], xt_bf[:, t * TC:(t + 1) * TC])
    b.wait("gpsimd", sP0S)
    b.collective("AllGather", ALU.bypass, ag_xt_in[:, :], XTG[:, :], sAGxt)
    arena.release()
    C.update(sAGxt=sAGxt, XTG=XTG, sP0S=sP0S)

    b.tap("xtg", XTG, W * D, TC, BF16, gate_sem=sAGxt)

    if "p1" in b.phases:
        build_p1(b, C)
    if "p2" in b.phases:
        build_d0(b, C)
        build_p2(b, C)
    if "p3" in b.phases:
        build_p3(b, C)
    if "p4" in b.phases:
        build_p4(b, C)

    b.finish()
    return b


def build_p1(b, C):
    b.pool_reset()
    arena, ps = C["arena"], C["ps"]
    sCst, sGP = C["sCst"], C["sGP"]
    sb_ones, sb_eye, sb_ones1f = C["sb_ones"], C["sb_eye"], C["sb_ones1f"]
    sb_consts = C["sb_consts"]
    XTG, sAGxt, xtf = C["XTG"], C["sAGxt"], C["xtf"]
    cEPSLN = sb_consts[0:1, 0:1]
    cNHALF = sb_consts[0:1, 2:3]
    cQSC = sb_consts[:, 3:4]

    wq = b.din("wq", [D, HPC * DH], F32)
    wk = b.din("wk", [D, HPC * DH], F32)
    wv = b.din("wv", [D, HPC * DH], F32)
    wo = b.din("wo", [D, D], F32)
    maskt = b.din("maskt", [KC, 4 * 128], F32)

    att_a2a_in = b.dint("att_a2a_in", [D, TC], BF16)
    ATTL = b.dint("ATTL", [D, TC], BF16)
    X1F = b.dint("X1F", [D, TC], F32)
    X1B = b.dint("X1B", [D, TC], BF16)
    bc1c = b.dint("p1c_bcd", [1, 2 * KC], F32)
    sA2Aatt = b.sem("sA2Aatt")

    arena.mark()
    xg_buf = [arena.alloc(NDT * KC, BF16) for _ in range(2)]
    w_stage = xg_buf[0][:, :].bitcast(F32)
    wq_bf = arena.alloc(NDT * HPC * DH, BF16)
    wk_bf = arena.alloc(NDT * HPC * DH, BF16)
    wv_bf = arena.alloc(NDT * HPC * DH, BF16)
    sWc = b.psem()
    sWL = b.psem()
    b.wait("sync", C["sP0S"])
    for src, dstv in ((wq, wq_bf), (wk, wk_bf), (wv, wv_bf)):
        for t in range(NDT):
            b.dma("sync", sWL, w_stage[:, t * 256:(t + 1) * 256], src[t * 128:(t + 1) * 128, :])
        b.wait("vector", sWL)
        sWc.n += 1
        b.on("vector", lambda e, s=w_stage, d=dstv, h=sWc.h: e.tensor_copy(out=d, in_=s).then_inc(h, 1))
        b.wait("sync", sWc)

    QT = [arena.alloc(T, BF16) for _ in range(HPC)]
    KT = [arena.alloc(T, BF16) for _ in range(HPC)]
    VSB = arena.alloc(64 * 256, BF16)
    sb_mask = arena.alloc(4 * KC, F32)
    for kt in range(4):
        b.dma("sync", sCst, sb_mask[:, kt * KC:(kt + 1) * KC], maskt[kt * 128:(kt + 1) * 128, :])
    rows_bf = arena.alloc(2 * T, BF16)
    rows_s = xg_buf[1]

    sG = b.psem()
    sEvP = [b.sem("sEv0"), b.sem("sEv1")]
    nuse = [0, 0]
    b.wait("sync", sAGxt, 1)
    b.wait("tensor", sAGxt, 1)
    b.waitr("tensor", sWc, 3)
    b.wait("tensor", sCst)
    b.wait("vector", sCst)

    sXg = b.psem2()
    g_idx = 0

    def qkv_group(kind, idx, ch, buf):
        nonlocal g_idx
        g = g_idx
        p = g % 2
        bank = ps[p]
        if nuse[p] >= 1:
            b.waitr("tensor", sEvP[p], nuse[p])
        if kind in ("q", "k"):
            h = idx
            wsrc = wq_bf if kind == "q" else wk_bf
            for mt in range(NDT):
                mm(b, bank[:, :], wsrc[:, mt * 256 + h * 128:mt * 256 + (h + 1) * 128],
                   buf[:, mt * KC:(mt + 1) * KC], mt == 0, mt == NDT - 1,
                   inc=(sG.h, 1) if mt == NDT - 1 else None)
            sG.n += 1
            dst = (QT if kind == "q" else KT)[h]
            out_ap = dst[:, ch * KC:(ch + 1) * KC]
            src_ap = bank[:, :]
            scale = QSC if kind == "q" else None
        else:
            st4 = idx
            for mt in range(NDT):
                mm(b, bank[:, 0:256], buf[:, mt * KC + st4 * 128:mt * KC + (st4 + 1) * 128],
                   wv_bf[:, mt * 256:(mt + 1) * 256], mt == 0, mt == NDT - 1,
                   inc=(sG.h, 1) if mt == NDT - 1 else None)
            sG.n += 1
            stg = ch * 4 + st4
            out_ap = VSB[:, stg * 256:(stg + 1) * 256]
            src_ap = bank[:, 0:256]
            scale = None
        eng = "scalar" if p == 0 else "vector"
        b.wait(eng, sG)
        sEvP[p].n += 1
        if eng == "scalar":
            if scale is not None:
                b.on("scalar", lambda e, o=out_ap, i=src_ap, h2=sEvP[p].h:
                     e.mul(out=o, in_=i, mul=cQSC).then_inc(h2, 1))
            else:
                b.on("scalar", lambda e, o=out_ap, i=src_ap, h2=sEvP[p].h:
                     e.copy(out=o, in_=i).then_inc(h2, 1))
        else:
            if scale is not None:
                b.on("vector", lambda e, o=out_ap, i=src_ap, h2=sEvP[p].h, s=scale:
                     e.tensor_scalar_mul(out=o, in0=i, scalar1=s).then_inc(h2, 1))
            else:
                b.on("vector", lambda e, o=out_ap, i=src_ap, h2=sEvP[p].h:
                     e.tensor_copy(out=o, in_=i).then_inc(h2, 1))
        nuse[p] += 1
        g_idx += 1

    for ch in range(NKC):
        buf = xg_buf[ch % 2]
        r, c0 = ch // 2, (ch % 2) * KC
        if ch >= 2:
            b.waitr("sync", sG, 8 * (ch - 1))
        elif ch == 0:
            b.waitr("sync", sWc, 3)
        for mt in range(NDT):
            b.dma("sync", sXg[ch % 2], buf[:, mt * KC:(mt + 1) * KC],
                  XTG[r * D + mt * 128:r * D + (mt + 1) * 128, c0:c0 + KC])
        b.wait("tensor", sXg[ch % 2])
        for h in range(HPC):
            qkv_group("q", h, ch, buf)
            qkv_group("k", h, ch, buf)
        for st4 in range(4):
            qkv_group("v", st4, ch, buf)

    qt2_bf = xg_buf[0]
    pS = [ps[0], ps[1]]
    pAcc = ps[2]
    pL = ps[3]
    pRow = [ps[4], ps[5]]
    pT_bf = ps[6][:, :].bitcast(BF16)

    sSq = b.psem()
    sQn2m = b.psem()
    sRowE = b.psem()
    sKmax = b.psem()
    sSqr = b.psem()
    sPad = b.psem()
    sPrep = b.psem()
    sS = b.psem()
    sM = b.psem()
    sExp = b.psem()
    sAV = b.psem()
    sLi = b.psem()
    sAT = b.psem()
    sTr = b.psem()
    sShard = b.psem()

    linv_sb = arena.alloc(1, F32)
    at_sb = arena.alloc(128, BF16)
    pt_sb = [arena.alloc(KC, BF16) for _ in range(2)]
    shard_st = [arena.alloc(128, BF16) for _ in range(2)]
    sShD = [b.sem("sShD0"), b.sem("sShD1")]

    cg = 0
    qtg = 0
    for h in range(HPC):
        if h == 0:
            b.wait("scalar", sG)
            b.wait("scalar", sEvP[0])
            b.wait("scalar", sEvP[1])
        else:
            b.wait("scalar", sS)
            b.wait("scalar", sQn2m)
        sSq.n += 1
        b.on("scalar", lambda e, o=qt2_bf, i=QT[h], hh=sSq.h: e.activation(out=o, in_=i, func=AF.Square).then_inc(hh, 1))
        b.wait("tensor", sSq)
        for part, rowbase in ((0, 0), (1, 64)):
            for chn in range(NKC):
                bank = pRow[chn % 2]
                if part * NKC + chn >= 2:
                    b.wait("tensor", sRowE, sRowE.n - 1)
                sQn2m.n += 1
                mm(b, bank[0:1, :], sb_ones, qt2_bf[:, chn * KC:(chn + 1) * KC],
                   True, True, inc=(sQn2m.h, 1))
                b.wait("scalar", sQn2m)
                sRowE.n += 1
                b.on("scalar", lambda e, o=rows_s[rowbase:rowbase + 1, chn * KC:(chn + 1) * KC],
                     i=bank[0:1, :], hh=sRowE.h: e.copy(out=o, in_=i).then_inc(hh, 1))
            if part == 0:
                b.wait("scalar", sQn2m)
                sSq.n += 1
                b.on("scalar", lambda e, o=qt2_bf, i=KT[h], hh=sSq.h: e.activation(out=o, in_=i, func=AF.Square).then_inc(hh, 1))
                b.wait("tensor", sSq)
        sSqr.n += 1
        b.on("scalar", lambda e, hh=sSqr.h: e.activation(out=rows_s[32:33, 0:T], in_=rows_s[0:1, 0:T], func=AF.Sqrt).then_inc(hh, 1))
        b.wait("vector", sSqr)
        b.on("vector", lambda e: e.tensor_scalar(out=rows_bf[0:1, 0:T], in0=rows_s[32:33, 0:T],
                                                 scalar1=-1.01, scalar2=-0.5, op0=ALU.mult, op1=ALU.add))
        b.wait("vector", sRowE)
        sKmax.n += 1
        b.on("vector", lambda e, hh=sKmax.h: e.tensor_reduce(
            out=rows_bf[32:33, 0:NKC], in_=rows_s[64:65, 0:T].rearrange("p (c k) -> p c k", c=NKC),
            axis=AX.X, op=ALU.max).then_inc(hh, 1))
        b.wait("scalar", sKmax)
        sSqr.n += 1
        b.on("scalar", lambda e, hh=sSqr.h: e.activation(out=rows_bf[32:33, NKC:2 * NKC], in_=rows_bf[32:33, 0:NKC], func=AF.Sqrt).then_inc(hh, 1))
        b.wait("vector", sSqr)
        sPad.n += 1
        b.on("vector", lambda e, hh=sPad.h: e.tensor_scalar(out=rows_bf[32:33, 2 * NKC:3 * NKC], in0=rows_bf[32:33, NKC:2 * NKC],
                                                 scalar1=1.01, scalar2=0.5, op0=ALU.mult, op1=ALU.add).then_inc(hh, 1))
        b.wait("vector", sPad)
        sPrep.n += 1
        b.on("vector", lambda e, hh=sPrep.h: e.tensor_copy(
            out=rows_bf[0:1, T:2 * T].rearrange("p (c k) -> p c k", c=NKC),
            in_=rows_bf[32:33, 2 * NKC:3 * NKC].rearrange("p (c o) -> p c o", o=1).broadcast_to([1, NKC, KC])).then_inc(hh, 1))
        b.wait("tensor", sPrep)

        pe_tail = []

        def emit_S(qt, ci, my_cg):
            bank = pS[my_cg % 2]
            if my_cg >= 2:
                b.waitr("tensor", sExp, my_cg - 1)
            diag = (ci == qt // 4)
            for j in range(4):
                kt = ci * 4 + j
                mm(b, bank[:, j * 128:(j + 1) * 128],
                   rows_bf[0:1, T + kt * 128:T + (kt + 1) * 128], rows_bf[0:1, qt * 128:(qt + 1) * 128],
                   True, False)
                mm(b, bank[:, j * 128:(j + 1) * 128],
                   KT[h][:, kt * 128:(kt + 1) * 128], QT[h][:, qt * 128:(qt + 1) * 128],
                   False, True, inc=(sS.h, 1) if j == 3 else None)
            sS.n += 1
            if diag:
                v = qt % 4
                b.wait("vector", sS)
                sM.n += 1
                b.on("vector", lambda e, pk=bank, vv=v, hh=sM.h: e.tensor_tensor(
                    out=pk[:, :].rearrange("p (a q) -> p a q", a=4),
                    in0=pk[:, :].rearrange("p (a q) -> p a q", a=4),
                    in1=sb_mask[:, :].rearrange("p (a vq) -> p a vq", a=4)[:, :, vv * 128:(vv + 1) * 128],
                    op=ALU.add).then_inc(hh, 1))
            if diag:
                b.wait("scalar", sM)
            else:
                b.wait("scalar", sS)
            if my_cg >= 2:
                b.waitr("scalar", sAV, my_cg - 1)
            sExp.n += 1
            b.on("scalar", lambda e, pk=bank, o=pt_sb[my_cg % 2], hh=sExp.h:
                 e.activation(out=o, in_=pk[:, :], func=AF.Exp).then_inc(hh, 1))

        def emit_AV(qt, ci, my_cg, nch, my_qtg):
            b.waitr("tensor", sExp, my_cg + 1)
            if ci == 0:
                b.waitr("tensor", sAT, my_qtg)
                b.waitr("tensor", sLi, my_qtg)
            pt = pt_sb[my_cg % 2]
            first = (ci == 0)
            last = (ci == nch - 1)
            for j in range(4):
                kt = ci * 4 + j
                mm(b, pAcc[:, 0:128], pt[:, j * 128:(j + 1) * 128],
                   VSB[:, kt * 256 + h * 128:kt * 256 + (h + 1) * 128],
                   first and j == 0, last and j == 3)
            for j in range(4):
                mm(b, pL[:, 0:1], pt[:, j * 128:(j + 1) * 128], sb_ones,
                   first and j == 0, last and j == 3,
                   inc=(sAV.h, 1) if j == 3 else None)
            sAV.n += 1

        def emit_tail(qt, my_qtg):
            b.wait("vector", sAV)
            b.waitr("vector", sAT, my_qtg)
            sLi.n += 1
            b.on("vector", lambda e, hh=sLi.h: e.reciprocal(out=linv_sb, in_=pL[:, 0:1]).then_inc(hh, 1))
            b.wait("scalar", sLi)
            b.waitr("scalar", sTr, my_qtg)
            sAT.n += 1
            b.on("scalar", lambda e, hh=sAT.h: e.mul(out=at_sb, in_=pAcc[:, 0:128], mul=linv_sb).then_inc(hh, 1))
            def pe_tr(e, hh=sTr.h):
                e.transpose(pT_bf[:, 0:128], at_sb, sb_eye).then_inc(hh, 1)
            sTr.n += 1
            tail_waits = (sAT.n, sShard.n)
            b.wait("vector", sTr)
            stg = shard_st[my_qtg % 2]
            if my_qtg >= 2:
                b.wait("vector", sShD[my_qtg % 2])
            sShard.n += 1
            b.on("vector", lambda e, o=stg, hh=sShard.h:
                 e.tensor_copy(out=o, in_=pT_bf[:, 0:128]).then_inc(hh, 1))
            jc = qt // 8
            col = (qt % 8) * 128
            b.wait("gpsimd", sShard)
            b.dma("gpsimd", sShD[my_qtg % 2],
                  att_a2a_in[jc * 256 + h * 128:jc * 256 + (h + 1) * 128, col:col + 128], stg)
            return (pe_tr, tail_waits)

        for qt in range(NQT):
            nch = qt // 4 + 1
            cgs = []
            for ci in range(nch):
                emit_S(qt, ci, cg)
                cgs.append(cg)
                cg += 1
                if pe_tail:
                    fn, (at_v, sh_v) = pe_tail.pop()
                    b.wait("tensor", sAT, at_v)
                    b.wait("tensor", sShard, sh_v)
                    b.on("tensor", fn)
                if ci >= 1:
                    emit_AV(qt, ci - 1, cgs[ci - 1], nch, qtg)
            emit_AV(qt, nch - 1, cgs[nch - 1], nch, qtg)
            pe_tail.append(emit_tail(qt, qtg))
            qtg += 1
        if pe_tail:
            fn, (at_v, sh_v) = pe_tail.pop()
            b.wait("tensor", sAT, at_v)
            b.wait("tensor", sShard, sh_v)
            b.on("tensor", fn)

    b.wait("gpsimd", sShD[0])
    b.wait("gpsimd", sShD[1])
    b.collective("AllToAll", ALU.bypass, att_a2a_in[:, :], ATTL[:, :], sA2Aatt)

    b.tap("attl", ATTL, D, TC, BF16, gate_sem=sA2Aatt)

    arena.release()
    arena.mark()
    wo_stage = [arena.alloc(D, F32) for _ in range(2)]
    wo_bf = arena.alloc(NDT * D, BF16)
    attl_sb = arena.alloc(NDT * TC, BF16)
    x1pre = arena.alloc(NDT * KC, F32)
    xres_buf = [arena.alloc(KC, F32) for _ in range(2)]
    sq_buf = [arena.alloc(KC, BF16) for _ in range(2)]
    xb_buf = [arena.alloc(KC, BF16) for _ in range(2)]
    rows2 = arena.alloc(8 * KC, F32)
    rstd_b = arena.alloc(KC, F32)
    mrow_b = arena.alloc(KC, F32)

    def R2(slot):
        return rows2[0:1, slot * KC:(slot + 1) * KC]

    sWoC = b.psem()
    sWoL = b.psem2()
    sAtl = b.psem()
    sXr = b.psem2()
    sStp = b.sem2("sStp")
    sPO = b.psem()
    sX1p = b.psem()
    sSqE = b.psem()
    sXbC = b.psem()
    sStat = b.psem()
    sRowC = b.psem()
    sBc = b.psem()
    sBcE = b.psem()
    sX1 = b.psem()
    sX1st = b.psem()

    b.waitr("sync", sA2Aatt, 1)
    for dht in range(NDT):
        stg = wo_stage[dht % 2]
        if dht >= 2:
            b.waitr("sync", sWoC, dht - 1)
        b.dma("sync", sWoL[dht % 2], stg, wo[dht * 128:(dht + 1) * 128, :])
        b.wait("vector", sWoL[dht % 2])
        sWoC.n += 1
        b.on("vector", lambda e, o=wo_bf[:, dht * D:(dht + 1) * D], i=stg, hh=sWoC.h:
             e.tensor_copy(out=o, in_=i).then_inc(hh, 1))
    for dht in range(NDT):
        b.dma("sync", sAtl, attl_sb[:, dht * TC:(dht + 1) * TC], ATTL[dht * 128:(dht + 1) * 128, :])

    b.wait("tensor", sWoC)
    b.wait("tensor", sAtl)

    pO = [ps[0], ps[1]]
    pSx, pSxx = ps[4], ps[5]
    pB1, pB2 = ps[6], ps[7]

    for tch in range(2):
        if tch >= 1:
            b.waitr("tensor", sRowC, 3 * tch)
        for dm in range(NDT):
            gi = tch * NDT + dm
            bank = pO[gi % 2]
            if gi >= 2:
                b.waitr("tensor", sX1p, gi - 1)
            for dht in range(NDT):
                mm(b, bank[:, :], wo_bf[:, dht * D + dm * 128:dht * D + (dm + 1) * 128],
                   attl_sb[:, dht * TC + tch * KC:dht * TC + tch * KC + KC],
                   dht == 0, dht == NDT - 1, inc=(sPO.h, 1) if dht == NDT - 1 else None)
            sPO.n += 1
            xrb = xres_buf[gi % 2]
            if gi >= 2:
                b.waitr("sync", sX1p, gi - 1)
            b.dma("sync", sXr[gi % 2], xrb, xtf[dm * 128:(dm + 1) * 128, tch * KC:(tch + 1) * KC])
            b.wait("vector", sPO)
            b.wait("vector", sXr[gi % 2])
            sX1p.n += 1
            b.on("vector", lambda e, o=x1pre[:, dm * KC:(dm + 1) * KC], i0=bank[:, :], i1=xrb, hh=sX1p.h:
                 e.tensor_tensor(out=o, in0=i0, in1=i1, op=ALU.add).then_inc(hh, 1))
            sqb = sq_buf[gi % 2]
            xbb = xb_buf[gi % 2]
            if gi >= 2:
                b.waitr("scalar", sStat, 2 * (gi - 1))
                b.waitr("vector", sStat, 2 * (gi - 1))
            b.wait("scalar", sX1p)
            sSqE.n += 1
            b.on("scalar", lambda e, o=sqb, i=x1pre[:, dm * KC:(dm + 1) * KC], hh=sSqE.h:
                 e.activation(out=o, in_=i, func=AF.Square).then_inc(hh, 1))
            b.wait("vector", sX1p)
            sXbC.n += 1
            b.on("vector", lambda e, o=xbb, i=x1pre[:, dm * KC:(dm + 1) * KC], hh=sXbC.h:
                 e.tensor_copy(out=o, in_=i).then_inc(hh, 1))
            b.wait("tensor", sXbC)
            mm(b, pSx[0:1, :], sb_ones, xbb, dm == 0, dm == NDT - 1,
               inc=(sStat.h, 1))
            b.wait("tensor", sSqE)
            mm(b, pSxx[0:1, :], sb_ones, sqb, dm == 0, dm == NDT - 1,
               inc=(sStat.h, 1))
            sStat.n += 2
        b.wait("vector", sStat)
        b.on("vector", lambda e: e.tensor_scalar_mul(out=R2(0), in0=pSx[0:1, :], scalar1=1.0 / D))
        b.on("vector", lambda e: e.tensor_scalar_mul(out=R2(1), in0=pSxx[0:1, :], scalar1=1.0 / D))
        b.on("vector", lambda e: e.tensor_tensor(out=R2(2), in0=R2(0), in1=R2(0), op=ALU.mult))
        sRowC.n += 1
        b.on("vector", lambda e, hh=sRowC.h: e.tensor_sub(out=R2(3), in0=R2(1), in1=R2(2)).then_inc(hh, 1))
        b.wait("scalar", sRowC)
        b.on("scalar", lambda e: e.activation(out=R2(4), in_=R2(3), func=AF.Ln, bias=cEPSLN))
        sRowC.n += 1
        b.on("scalar", lambda e, hh=sRowC.h: e.activation(out=R2(5), in_=R2(4), func=AF.Exp, scale=cNHALF).then_inc(hh, 1))
        b.wait("vector", sRowC)
        sRowC.n += 1
        b.on("vector", lambda e, hh=sRowC.h: e.tensor_tensor(out=R2(6), in0=R2(0), in1=R2(5), op=ALU.mult).then_inc(hh, 1))
        b.wait("sync", sRowC)
        b.dma("sync", sBc, bc1c[0:1, :], rows2[0:1, 5 * KC:7 * KC])
        b.wait("sync", sBc)
        b.dma("sync", sBcE, rstd_b, bc1c[0:1, 0:KC].broadcast_to([128, KC]))
        b.dma("sync", sBcE, mrow_b, bc1c[0:1, KC:2 * KC].broadcast_to([128, KC]))
        b.wait("vector", sBcE)
        for dm in range(NDT):
            src = x1pre[:, dm * KC:(dm + 1) * KC]
            t0f = xres_buf[dm % 2]
            if dm >= 2:
                b.wait("vector", sStp[dm % 2])
            sX1.n += 1
            b.on("vector", lambda e, o=t0f, i=src: e.tensor_tensor(out=o, in0=i, in1=rstd_b, op=ALU.mult))
            b.on("vector", lambda e, o=t0f, i=t0f, hh=sX1.h: e.tensor_sub(out=o, in0=i, in1=mrow_b).then_inc(hh, 1))
            b.wait("sync", sX1)
            b.dma("sync", sStp[dm % 2], X1F[dm * 128:(dm + 1) * 128, tch * KC:(tch + 1) * KC], t0f)
            xbc = sq_buf[dm % 2]
            b.wait("vector", sX1)
            sX1.n += 1
            b.on("vector", lambda e, o=xbc, i=t0f, hh=sX1.h: e.tensor_copy(out=o, in_=i).then_inc(hh, 1))
            b.wait("sync", sX1)
            b.dma("sync", sStp[dm % 2], X1B[dm * 128:(dm + 1) * 128, tch * KC:(tch + 1) * KC], xbc)
        b.wait("sync", sStp[0])
        b.wait("sync", sStp[1])

    b.tap("x1f", X1F, D, TC, F32, gate_sem=sX1st)
    b.tap("x1b", X1B, D, TC, BF16, gate_sem=sX1st)
    C["X1F"], C["X1B"] = X1F, X1B
    C["sStp"] = sStp
    arena.release()


def emit_lut16(b, idxb, acc, tmp, lut_col, sDq):
    b.on("vector", lambda e, o=acc, i=idxb, s2=lut_col(0): e.tensor_scalar(
        out=o, in0=i, scalar1=0.0, scalar2=s2, op0=ALU.is_equal, op1=ALU.mult))
    for v in range(1, 16):
        b.on("vector", lambda e, o=tmp, i=idxb, vv=float(v), s2=lut_col(v): e.tensor_scalar(
            out=o, in0=i, scalar1=vv, scalar2=s2, op0=ALU.is_equal, op1=ALU.mult))
        if v < 15:
            b.on("vector", lambda e, o=acc, i=tmp: e.tensor_add(out=o, in0=o, in1=i))
        else:
            sDq.n += 1
            b.on("vector", lambda e, o=acc, i=tmp, hh=sDq.h: e.tensor_add(out=o, in0=o, in1=i).then_inc(hh, 1))


def build_d0(b, C):
    b.pool_reset()
    """L0 ghost-FFN dequant (this core's shard) + AllGathers."""
    arena, ps = C["arena"], C["ps"]
    sCst, sGP = C["sCst"], C["sGP"]

    g_idxT = b.din("g_idxT", [HPC * DH, I0], I32)
    u_idxT = b.din("u_idxT", [HPC * DH, I0], I32)
    d_idxT = b.din("d_idxT", [688, D], I32)
    lutb = b.din("lutb", [128, 16], F32)

    NT0 = I0 // 128
    ag_w0gu_in = b.dint("ag_w0gu_in", [2 * 2 * NT0 * 128, 128], BF16)
    W0GUT = b.dint("W0GUT", [W * 2 * 2 * NT0 * 128, 128], BF16)
    ag_w0d_in = b.dint("ag_w0d_in", [688, D], BF16)
    W0DT = b.dint("W0DT", [I0, D], BF16)
    sAGw0 = b.sem("sAGw0")

    arena.mark()
    sb_lut = arena.alloc(16, F32)
    idx_st = [arena.alloc(1024, I32) for _ in range(2)]
    idxb = arena.alloc(1024, BF16)
    tmp = arena.alloc(1024, BF16)
    acc = [arena.alloc(1024, BF16) for _ in range(2)]
    sDqL = b.psem2()
    sDq = b.psem()
    sDqS = [b.sem("sDqS0"), b.sem("sDqS1")]
    sLut = b.psem()

    for p in (0, 1):
        b.wait("sync", C["sStp"][p])
    b.dma("sync", sLut, sb_lut, lutb[:, :])

    def lut_col(v):
        return sb_lut[:, v:v + 1]

    nchunk = 0

    def dq_chunk(src_hbm, r0, pr, c0, cols, store_fn):
        nonlocal nchunk
        p = nchunk % 2
        ist = idx_st[p]
        if nchunk >= 2:
            b.waitr("sync", sDq, nchunk - 1)
        b.dma("sync", sDqL[p], ist[0:pr, 0:cols], src_hbm[r0:r0 + pr, c0:c0 + cols])
        b.wait("vector", sDqL[p])
        if nchunk == 0:
            b.wait("vector", sLut)
        a = acc[p]
        if nchunk >= 2:
            b.wait("vector", sDqS[p])
        b.on("vector", lambda e, o=idxb[0:pr, 0:cols], i=ist[0:pr, 0:cols]: e.tensor_copy(out=o, in_=i))
        emit_lut16(b, idxb[0:pr, 0:cols], a[0:pr, 0:cols], tmp[0:pr, 0:cols],
                   lambda v, pr=pr: sb_lut[0:pr, v:v + 1], sDq)
        b.wait("gpsimd", sDq)
        store_fn(a, sDqS[p])
        nchunk += 1

    for src, base_it in ((g_idxT, 0), (u_idxT, NT0)):
        for mtl in range(2):
            it0 = 0
            while it0 < NT0:
                csz = min(8, NT0 - it0)
                def store(a, ssem, mtl=mtl, base_it=base_it, it0=it0, csz=csz):
                    for t in range(csz):
                        row = (mtl * 2 * NT0 + base_it + it0 + t) * 128
                        b.dma("gpsimd", ssem,
                              ag_w0gu_in[row:row + 128, :], a[:, t * 128:(t + 1) * 128])
                dq_chunk(src, mtl * 128, 128, it0 * 128, csz * 128, store)
                it0 += csz
    for rg in range(6):
        r0 = rg * 128
        pr = min(128, 688 - r0)
        for half in range(2):
            def store(a, ssem, r0=r0, pr=pr, half=half):
                b.dma("gpsimd", ssem, ag_w0d_in[r0:r0 + pr, half * 1024:(half + 1) * 1024],
                      a[0:pr, 0:1024])
            dq_chunk(d_idxT, r0, pr, half * 1024, 1024, store)

    b.wait("gpsimd", sDqS[0])
    b.wait("gpsimd", sDqS[1])
    b.collective("AllGather", ALU.bypass, ag_w0gu_in[:, :], W0GUT[:, :], sAGw0)
    b.collective("AllGather", ALU.bypass, ag_w0d_in[:, :], W0DT[:, :], sAGw0)
    arena.release()
    C.update(W0GUT=W0GUT, W0DT=W0DT, sAGw0=sAGw0)
    b.tap("w0gut", W0GUT, 4096, 128, BF16)
    b.tap("w0dt", W0DT, I0, D, BF16)


class LnCtx:

    def __init__(self, b, C, name):
        arena, ps = C["arena"], C["ps"]
        self.b = b
        self.C = C
        self.xpre = arena.alloc(NDT * KC, F32)
        self.res_buf = [arena.alloc(KC, F32) for _ in range(2)]
        self.sq_buf = [arena.alloc(KC, BF16) for _ in range(2)]
        self.xb_buf = [arena.alloc(KC, BF16) for _ in range(2)]
        self.rows = arena.alloc(4 * KC, F32)
        self.rstd_b = arena.alloc(KC, F32)
        self.mrow_b = arena.alloc(KC, F32)
        self.sAdd = b.psem()
        self.sSqE = b.psem()
        self.sXbC = b.psem()
        self.sStat = b.psem()
        self.sRowC = b.psem()
        self.sBc = b.psem()
        self.sBcE = b.psem()
        self.sX = b.psem()
        self.sSt = b.psem2()
        self.sLr = b.psem2()
        self.bc_dram = b.dint(name + "_bcd", [1, 2 * KC], F32)

    def R(self, slot):
        return self.rows[0:1, slot * KC:(slot + 1) * KC]

    def add_tile(self, dm, pre_ap, res_hbm, rescol):
        b = self.b
        xrb = self.res_buf[dm % 2]
        if self.sAdd.n >= 2:
            b.wait("sync", self.sAdd, self.sAdd.n - 1)
        b.dma("sync", self.sLr[dm % 2], xrb, res_hbm[dm * 128:(dm + 1) * 128, rescol:rescol + KC])
        b.wait("vector", self.sLr[dm % 2])
        self.sAdd.n += 1
        b.on("vector", lambda e, o=self.xpre[:, dm * KC:(dm + 1) * KC], i0=pre_ap, i1=xrb, hh=self.sAdd.h:
             e.tensor_tensor(out=o, in0=i0, in1=i1, op=ALU.add).then_inc(hh, 1))

    def stats_norm_store(self, outs, rms=False, dbg=None):
        b = self.b
        C = self.C
        if dbg is not None:
            hbm, col0 = dbg
            b.wait("sync", self.sAdd)
            for dm in range(NDT):
                b.dma("sync", self.sSt[0], hbm[dm * 128:(dm + 1) * 128, col0:col0 + KC],
                      self.xpre[:, dm * KC:(dm + 1) * KC])
            b.wait("sync", self.sSt[0])
        ps = C["ps"]
        sb_ones, sb_ones1f = C["sb_ones"], C["sb_ones1f"]
        cEPS = C["sb_consts"][0:1, 0:1] if not rms else C["sb_consts"][0:1, 1:2]
        cNHALF = C["sb_consts"][0:1, 2:3]
        pSx, pSxx = ps[4], ps[5]
        pB1, pB2 = ps[6], ps[7]
        R = self.R
        b.wait("scalar", self.sAdd)
        b.wait("vector", self.sAdd)
        for dm in range(NDT):
            sqb = self.sq_buf[dm % 2]
            xbb = self.xb_buf[dm % 2]
            if dm >= 2:
                b.wait("scalar", self.sStat, self.sStat.n - (2 if not rms else 1))
                if not rms:
                    b.wait("vector", self.sStat, self.sStat.n - 2)
            src = self.xpre[:, dm * KC:(dm + 1) * KC]
            self.sSqE.n += 1
            b.on("scalar", lambda e, o=sqb, i=src, hh=self.sSqE.h:
                 e.activation(out=o, in_=i, func=AF.Square).then_inc(hh, 1))
            if not rms:
                self.sXbC.n += 1
                b.on("vector", lambda e, o=xbb, i=src, hh=self.sXbC.h:
                     e.tensor_copy(out=o, in_=i).then_inc(hh, 1))
                b.wait("tensor", self.sXbC)
                mm(b, pSx[0:1, :], sb_ones, xbb, dm == 0, dm == NDT - 1, inc=(self.sStat.h, 1))
                self.sStat.n += 1
            b.wait("tensor", self.sSqE)
            mm(b, pSxx[0:1, :], sb_ones, sqb, dm == 0, dm == NDT - 1, inc=(self.sStat.h, 1))
            self.sStat.n += 1
        b.wait("vector", self.sStat)
        if not rms:
            b.on("vector", lambda e: e.tensor_scalar_mul(out=R(0), in0=pSx[0:1, :], scalar1=1.0 / D))
        b.on("vector", lambda e: e.tensor_scalar_mul(out=R(1), in0=pSxx[0:1, :], scalar1=1.0 / D))
        if not rms:
            b.on("vector", lambda e: e.tensor_tensor(out=R(2), in0=R(0), in1=R(0), op=ALU.mult))
            self.sRowC.n += 1
            b.on("vector", lambda e, hh=self.sRowC.h: e.tensor_sub(out=R(3), in0=R(1), in1=R(2)).then_inc(hh, 1))
        else:
            self.sRowC.n += 1
            b.on("vector", lambda e, hh=self.sRowC.h: e.tensor_copy(out=R(3), in_=R(1)).then_inc(hh, 1))
        b.wait("scalar", self.sRowC)
        b.on("scalar", lambda e: e.activation(out=R(1), in_=R(3), func=AF.Ln, bias=cEPS))
        self.sRowC.n += 1
        b.on("scalar", lambda e, hh=self.sRowC.h: e.activation(out=R(2), in_=R(1), func=AF.Exp, scale=cNHALF).then_inc(hh, 1))
        b.wait("vector", self.sRowC)
        if not rms:
            self.sRowC.n += 1
            b.on("vector", lambda e, hh=self.sRowC.h: e.tensor_tensor(out=R(3), in0=R(0), in1=R(2), op=ALU.mult).then_inc(hh, 1))
        b.wait("sync", self.sRowC)
        b.dma("sync", self.sBc, self.bc_dram[0:1, 0:(KC if rms else 2 * KC)],
              self.rows[0:1, 2 * KC:(3 * KC if rms else 4 * KC)])
        b.wait("sync", self.sBc)
        b.dma("sync", self.sBcE, self.rstd_b, self.bc_dram[0:1, 0:KC].broadcast_to([128, KC]))
        if not rms:
            b.dma("sync", self.sBcE, self.mrow_b, self.bc_dram[0:1, KC:2 * KC].broadcast_to([128, KC]))
        b.wait("vector", self.sBcE)
        for dm in range(NDT):
            src = self.xpre[:, dm * KC:(dm + 1) * KC]
            t0f = self.res_buf[dm % 2]
            if dm >= 2:
                b.wait("vector", self.sSt[dm % 2])
            self.sX.n += 1
            if not rms:
                b.on("vector", lambda e, o=t0f, i=src: e.tensor_tensor(out=o, in0=i, in1=self.rstd_b, op=ALU.mult))
                b.on("vector", lambda e, o=t0f, hh=self.sX.h: e.tensor_sub(out=o, in0=o, in1=self.mrow_b).then_inc(hh, 1))
            else:
                b.on("vector", lambda e, o=t0f, i=src, hh=self.sX.h: e.tensor_tensor(out=o, in0=i, in1=self.rstd_b, op=ALU.mult).then_inc(hh, 1))
            need_bf = any(dt == BF16 for _, _, dt in outs)
            xbc = self.sq_buf[dm % 2]
            if need_bf:
                b.wait("vector", self.sX)
                self.sX.n += 1
                b.on("vector", lambda e, o=xbc, i=t0f, hh=self.sX.h: e.tensor_copy(out=o, in_=i).then_inc(hh, 1))
            b.wait("sync", self.sX)
            for hbm, col0, dt in outs:
                srcv = t0f if dt == F32 else xbc
                b.dma("sync", self.sSt[dm % 2], hbm[dm * 128:(dm + 1) * 128, col0:col0 + KC], srcv)
        b.wait("sync", self.sSt[0])
        b.wait("sync", self.sSt[1])

    def rms_scale_inplace(self):
        b = self.b
        C = self.C
        ps = C["ps"]
        sb_ones = C["sb_ones"]
        cEPS = C["sb_consts"][0:1, 1:2]
        cNHALF = C["sb_consts"][0:1, 2:3]
        pSxx = ps[5]
        R = self.R
        b.wait("scalar", self.sAdd)
        for dm in range(NDT):
            sqb = self.sq_buf[dm % 2]
            if dm >= 2:
                b.wait("scalar", self.sStat, self.sStat.n - 1)
            src = self.xpre[:, dm * KC:(dm + 1) * KC]
            self.sSqE.n += 1
            b.on("scalar", lambda e, o=sqb, i=src, hh=self.sSqE.h:
                 e.activation(out=o, in_=i, func=AF.Square).then_inc(hh, 1))
            b.wait("tensor", self.sSqE)
            mm(b, pSxx[0:1, :], sb_ones, sqb, dm == 0, dm == NDT - 1, inc=(self.sStat.h, 1))
            self.sStat.n += 1
        b.wait("vector", self.sStat)
        b.on("vector", lambda e: e.tensor_scalar_mul(out=R(1), in0=pSxx[0:1, :], scalar1=1.0 / D))
        self.sRowC.n += 1
        b.on("vector", lambda e, hh=self.sRowC.h: e.tensor_copy(out=R(3), in_=R(1)).then_inc(hh, 1))
        b.wait("scalar", self.sRowC)
        b.on("scalar", lambda e: e.activation(out=R(1), in_=R(3), func=AF.Ln, bias=cEPS))
        self.sRowC.n += 1
        b.on("scalar", lambda e, hh=self.sRowC.h:
             e.activation(out=R(2), in_=R(1), func=AF.Exp, scale=cNHALF).then_inc(hh, 1))
        b.wait("sync", self.sRowC)
        b.dma("sync", self.sBc, self.bc_dram[0:1, 0:KC], self.rows[0:1, 2 * KC:3 * KC])
        b.wait("sync", self.sBc)
        b.dma("sync", self.sBcE, self.rstd_b, self.bc_dram[0:1, 0:KC].broadcast_to([128, KC]))
        b.wait("vector", self.sBcE)
        for dm in range(NDT):
            src = self.xpre[:, dm * KC:(dm + 1) * KC]
            b.on("vector", lambda e, o=src, i=src:
                 e.tensor_tensor(out=o, in0=i, in1=self.rstd_b, op=ALU.mult))


def build_p2(b, C):
    b.pool_reset()
    """L0 ghost FFN: gate/up (silu*), down, +residual, LN."""
    arena, ps = C["arena"], C["ps"]
    sGP = C["sGP"]
    W0GUT, W0DT, sAGw0 = C["W0GUT"], C["W0DT"], C["sAGw0"]
    X1F, X1B = C["X1F"], C["X1B"]
    NT0 = I0 // 128

    X2F = b.dint("X2F", [D, TC], F32)
    X2PRE = b.dint("X2PRE", [D, TC], F32)
    ag_xg_in = b.dint("ag_xg_in", [D, TC], BF16)
    XG = b.dint("XG", [W * D, TC], BF16)
    sAGxg = b.sem("sAGxg")

    W0DTt = b.dint("W0DTt", [NT0 * 16 * 128, 128], BF16)
    sRpk = b.sem("sRpk")

    arena.mark()
    xb1_sb = arena.alloc(NDT * TC, BF16)
    H1 = arena.alloc(NT0 * TC, BF16)
    ln = LnCtx(b, C, "ln2")
    arena.mark()
    wbuf = [arena.alloc(32 * 128, BF16) for _ in range(2)]
    sg_buf = [arena.alloc(KC, BF16) for _ in range(2)]

    sP2g = b.psem()
    sSil = b.psem()
    sHE = b.psem()
    sDnG = b.psem()
    sX1bL = b.psem()
    sW0L = b.psem2()
    sWcL = b.psem2()

    b.wait("gpsimd", sAGw0, 2)
    for it in range(NT0):
        for hf in range(2):
            r0 = it * 16 * 128 + hf * 8 * 128
            b.dma("gpsimd", sRpk,
                  W0DTt[r0:r0 + 8 * 128, :].rearrange("(a p) c -> a p c", p=128),
                  W0DT[it * 128:(it + 1) * 128, hf * 1024:(hf + 1) * 1024].rearrange("p (a c) -> a p c", a=8))

    b.wait("sync", sAGw0, 2)
    for mt in range(NDT):
        b.dma("sync", sX1bL, xb1_sb[:, mt * TC:(mt + 1) * TC], X1B[mt * 128:(mt + 1) * 128, :])
    b.wait("tensor", sX1bL)

    for it in range(NT0):
        wb = wbuf[it % 2]
        if it >= 2:
            b.waitr("sync", sP2g, 4 * (it - 1))
        for mt in range(NDT):
            row = (mt * 2 * NT0 + it) * 128
            b.dma("sync", sW0L[it % 2], wb[:, mt * 128:(mt + 1) * 128], W0GUT[row:row + 128, :])
        for mt in range(NDT):
            row = (mt * 2 * NT0 + NT0 + it) * 128
            b.dma("sync", sW0L[it % 2], wb[:, (16 + mt) * 128:(17 + mt) * 128], W0GUT[row:row + 128, :])
        b.wait("tensor", sW0L[it % 2])
        for tch in range(2):
            for gu in range(2):
                bank = ps[(it % 2) * 4 + tch * 2 + gu]
                if it >= 2:
                    if gu == 0:
                        b.waitr("tensor", sSil, 2 * (it - 1))
                    else:
                        b.waitr("tensor", sHE, 2 * (it - 1))
                for mt in range(NDT):
                    mm(b, bank[:, :], wb[:, (gu * 16 + mt) * 128:(gu * 16 + mt + 1) * 128],
                       xb1_sb[:, mt * TC + tch * KC:mt * TC + tch * KC + KC],
                       mt == 0, mt == NDT - 1, inc=(sP2g.h, 1) if mt == NDT - 1 else None)
                sP2g.n += 1
        for tch in range(2):
            gbank = ps[(it % 2) * 4 + tch * 2 + 0]
            ubank = ps[(it % 2) * 4 + tch * 2 + 1]
            sgb = sg_buf[tch]
            k = 2 * it + tch
            if k >= 2:
                b.waitr("scalar", sHE, k - 1)
            b.waitr("scalar", sP2g, 4 * it + tch * 2 + 1)
            sSil.n += 1
            b.on("scalar", lambda e, o=sgb, i=gbank[:, :], hh=sSil.h:
                 e.activation(out=o, in_=i, func=AF.Silu).then_inc(hh, 1))
            b.wait("vector", sSil)
            b.waitr("vector", sP2g, 4 * it + tch * 2 + 2)
            sHE.n += 1
            b.on("vector", lambda e, o=H1[:, it * TC + tch * KC:it * TC + tch * KC + KC],
                 i0=sgb, i1=ubank[:, :], hh=sHE.h:
                 e.tensor_tensor(out=o, in0=i0, in1=i1, op=ALU.mult).then_inc(hh, 1))

    if os.environ.get("P2_GATEUP_ONLY"):
        b.wait("sync", sHE)
        arena.release()
        C.update(X2F=X2F, XG=XG, sAGxg=sAGxg)
        return
    arena.release()
    wcol = [arena.alloc(NT0 * 128, BF16) for _ in range(2)]
    b.wait("tensor", sHE)
    b.wait("sync", sRpk)
    b.wait("sync", sP2g)
    b.wait("sync", sHE)
    b.wait("scalar", sRpk)
    b.wait("scalar", sP2g)
    b.wait("scalar", sHE)
    for tch in range(2):
        if tch >= 1:
            b.wait("tensor", ln.sBcE)
        for g in range(2):
            if tch + g > 0:
                b.wait("tensor", ln.sAdd)
            for dml in range(8):
                dm = g * 8 + dml
                gidx = (tch * 2 + g) * 8 + dml
                wc = wcol[gidx % 2]
                q = "sync" if gidx % 2 == 0 else "scalar"
                qs = sWcL[gidx % 2]
                if gidx >= 2:
                    b.waitr(q, sDnG, gidx - 1)
                for i in range(NT0):
                    b.dma(q, qs, wc[:, i * 128:(i + 1) * 128],
                          W0DTt[(i * 16 + dm) * 128:(i * 16 + dm + 1) * 128, :])
                b.wait("tensor", qs)
                for i in range(NT0):
                    mm(b, ps[dml][:, :], wc[:, i * 128:(i + 1) * 128],
                       H1[:, i * TC + tch * KC:i * TC + tch * KC + KC],
                       i == 0, i == NT0 - 1,
                       inc=(sDnG.h, 1) if i == NT0 - 1 else None)
                sDnG.n += 1
            b.wait("vector", sDnG)
            for dml in range(8):
                dm = g * 8 + dml
                ln.add_tile(dm, ps[dml][:, :], X1F, tch * KC)
        ln.stats_norm_store([(X2F, tch * KC, F32), (ag_xg_in, tch * KC, BF16)],
                            dbg=(X2PRE, tch * KC) if "x2pre" in b.taps else None)
    b.wait("gpsimd", ln.sSt[0])
    b.wait("gpsimd", ln.sSt[1])
    b.collective("AllGather", ALU.bypass, ag_xg_in[:, :], XG[:, :], sAGxg)
    arena.release()
    C.update(X2F=X2F, XG=XG, sAGxg=sAGxg)
    b.tap("x2f", X2F, D, TC, F32, gate_sem=ln.sSt[0])
    b.tap("x2pre", X2PRE, D, TC, F32, gate_sem=ln.sSt[0])
    b.tap("xgin", ag_xg_in, D, TC, BF16, gate_sem=ln.sSt[0])


def build_p3(b, C):
    b.pool_reset()
    """L1 Mamba: xz proj, conv, dbc AllReduce, selective scan, out-proj, LN."""
    arena, ps = C["arena"], C["ps"]
    sGP = C["sGP"]
    XG, sAGxg = C["XG"], C["sAGxg"]
    X2F = C["X2F"]

    winxT = b.din("winxT", [D, EC], F32)
    winzT = b.din("winzT", [D, EC], F32)
    convw4 = b.din("convw4", [128, 16], F32)
    convb4 = b.din("convb4", [128, 4], F32)
    wxT = b.din("wxT", [EC, 160], F32)
    wdtT = b.din("wdtT", [RDT, EC], F32)
    bdt4 = b.din("bdt4", [128, 4], F32)
    alog4 = b.din("alog4", [128, 4 * NS], F32)
    dp4 = b.din("dp4", [128, 4], F32)
    woutT_t = b.din("woutT_t", [EC * 16, 128], F32)

    DBCP = b.dint("DBCP", [160, T], F32)
    DBC = b.dint("DBC", [160, T], F32)
    BCB = b.dint("BCB", [2 * NS, T], BF16)
    ag_yg_in = b.dint("ag_yg_in", [E4, TC], BF16)
    YGL = b.dint("YGL", [E4, TC], BF16)
    ag_wout_in = b.dint("ag_wout_in", [EC * 16, 128], BF16)
    WOUTT = b.dint("WOUTT", [E4 * 16, 128], BF16)
    X3F = b.dint("X3F", [D, TC], F32)
    X3B = b.dint("X3B", [D, TC], BF16)
    sARdbc = b.sem("sARdbc")
    sA2Ayg = b.sem("sA2Ayg")
    sAGwo2 = b.sem("sAGwo2")

    arena.mark()
    winz_bf = arena.alloc(NDT * KC, BF16)
    wdt_bf = arena.alloc(EC, BF16)
    A4 = arena.alloc(4 * NS, F32)
    sb_convw = arena.alloc(16, F32)
    sb_convb = arena.alloc(4, F32)
    sb_bdt = arena.alloc(4, F32)
    sb_alog = arena.alloc(4 * NS, F32)
    sb_dp = arena.alloc(4, F32)
    XC = arena.alloc(4 * T, BF16)
    xab = [arena.alloc(NDT * KC, BF16) for _ in range(2)]

    sWp = b.psem()
    sWoS = b.psem2()
    sWpL = b.psem2()
    sVec = b.psem()

    arena.mark()
    stg = [arena.alloc(KC, F32) for _ in range(2)]
    winx_bf = arena.alloc(NDT * KC, BF16)
    wx_bf = arena.alloc(4 * 160, BF16)
    wob_st = [arena.alloc(128, BF16) for _ in range(2)]
    xp_t = [arena.alloc(520, BF16) for _ in range(4)]
    accv2 = [arena.alloc(KC, F32) for _ in range(2)]
    dbcb = [arena.alloc(KC, F32) for _ in range(2)]
    dbcb2 = [arena.alloc(KC, F32) for _ in range(2)]

    b.wait("sync", sAGxg, 1)
    def cast_load(src_ap, dst_view, cols):
        if sWp.n >= 2:
            b.wait("sync", sWp, sWp.n - 1)
        p = sWp.n % 2
        sg = stg[p]
        b.dma("sync", sWpL[p], sg[:, 0:cols], src_ap)
        b.wait("vector", sWpL[p])
        sWp.n += 1
        b.on("vector", lambda e, o=dst_view, i=sg[:, 0:cols], hh=sWp.h:
             e.tensor_copy(out=o, in_=i).then_inc(hh, 1))

    for mt in range(NDT):
        cast_load(winxT[mt * 128:(mt + 1) * 128, :], winx_bf[:, mt * KC:(mt + 1) * KC], KC)
    for mt in range(NDT):
        cast_load(winzT[mt * 128:(mt + 1) * 128, :], winz_bf[:, mt * KC:(mt + 1) * KC], KC)
    for et in range(4):
        cast_load(wxT[et * 128:(et + 1) * 128, :], wx_bf[:, et * 160:(et + 1) * 160], 160)
    cast_load(wdtT[:, :], wdt_bf, EC)
    b.dma("sync", sVec, sb_convw, convw4[:, :])
    b.dma("sync", sVec, sb_convb, convb4[:, :])
    b.dma("sync", sVec, sb_bdt, bdt4[:, :])
    b.dma("sync", sVec, sb_alog, alog4[:, :])
    b.dma("sync", sVec, sb_dp, dp4[:, :])
    b.wait("scalar", sVec)
    sWp.n += 1
    b.on("scalar", lambda e, hh=sWp.h: e.activation(out=A4, in_=sb_alog, func=AF.Exp).then_inc(hh, 1))
    b.wait("vector", sWp)
    sWp.n += 1
    b.on("vector", lambda e, hh=sWp.h: e.tensor_scalar_mul(out=A4, in0=A4, scalar1=-1.0).then_inc(hh, 1))
    for i in range(EC * 16 // 128):
        p = i % 2
        wst = wob_st[p]
        if i >= 2:
            b.wait("sync", sWp, sWp.n - 1)
            b.wait("vector", sWoS[p])
        sg = stg[p]
        b.dma("sync", sWpL[p], sg[:, 0:128], woutT_t[i * 128:(i + 1) * 128, :])
        b.wait("vector", sWpL[p])
        sWp.n += 1
        b.on("vector", lambda e, o=wst, i2=sg[:, 0:128], hh=sWp.h:
             e.tensor_copy(out=o, in_=i2).then_inc(hh, 1))
        b.wait("sync", sWp)
        b.dma("sync", sWoS[p], ag_wout_in[i * 128:(i + 1) * 128, :], wst)

    sXP = b.psem()
    sXPe = b.psem()
    sCv = b.psem()
    sXC = b.psem()
    sDB = b.psem()
    sDBe = b.psem()
    sDBs = [b.sem("sDBs0"), b.sem("sDBs1")]
    sXa = b.psem2()

    for et in range(4):
        b.on("vector", lambda e, o=xp_t[et][:, 0:3]: e.memset(o, 0.0))
    b.wait("tensor", sWp)

    for ch in range(NKC):
        xb = xab[ch % 2]
        if ch >= 2:
            b.waitr("sync", sXP, 4 * (ch - 1))
        r, c0 = ch // 2, (ch % 2) * KC
        for mt in range(NDT):
            b.dma("sync", sXa[ch % 2], xb[:, mt * KC:(mt + 1) * KC],
                  XG[r * D + mt * 128:r * D + (mt + 1) * 128, c0:c0 + KC])
        b.wait("tensor", sXa[ch % 2])
        for et in range(4):
            g = 4 * ch + et
            bank = ps[g % 2]
            if g >= 2:
                b.waitr("tensor", sXPe, g - 1)
            for mt in range(NDT):
                mm(b, bank[:, :], winx_bf[:, mt * KC + et * 128:mt * KC + (et + 1) * 128],
                   xb[:, mt * KC:(mt + 1) * KC], mt == 0, mt == NDT - 1,
                   inc=(sXP.h, 1) if mt == NDT - 1 else None)
            sXP.n += 1
            b.wait("vector", sXP)
            xpt = xp_t[et]
            if ch > 0:
                b.on("vector", lambda e, o=xpt[:, 0:3], i=xpt[:, 512:515]: e.tensor_copy(out=o, in_=i))
            sXPe.n += 1
            b.on("vector", lambda e, o=xpt[:, 3:515], i=bank[:, :], hh=sXPe.h:
                 e.tensor_copy(out=o, in_=i).then_inc(hh, 1))
            accv = accv2[g % 2]
            if g >= 2:
                b.waitr("vector", sXC, g - 1)
            b.on("vector", lambda e, o=accv, i=xpt[:, 0:512], s1=sb_convw[:, et * 4:et * 4 + 1],
                 s2=sb_convb[:, et:et + 1]:
                 e.tensor_scalar(out=o, in0=i, scalar1=s1, scalar2=s2, op0=ALU.mult, op1=ALU.add))
            for k in range(1, 4):
                inc = (sCv, 1) if k == 3 else None
                if inc:
                    sCv.n += 1
                def cop(e, o=accv, i=xpt[:, k:512 + k], s=sb_convw[:, et * 4 + k:et * 4 + k + 1], ii=inc):
                    ins = e.scalar_tensor_tensor(out=o, in0=i, scalar=s, in1=o, op0=ALU.mult, op1=ALU.add)
                    if ii:
                        ins.then_inc(ii[0].h, ii[1])
                b.on("vector", cop)
            b.wait("scalar", sCv)
            sXC.n += 1
            b.on("scalar", lambda e, o=XC[:, et * T + ch * KC:et * T + ch * KC + KC], i=accv, hh=sXC.h:
                 e.activation(out=o, in_=i, func=AF.Silu).then_inc(hh, 1))
        b.wait("tensor", sXC)
        for fp in range(2):
            bank = ps[4 + fp]
            pr = 128 if fp == 0 else 32
            if ch >= 2:
                b.waitr("tensor", sDBe, 2 * (ch - 1))
            for et in range(4):
                lh = wx_bf[:, et * 160:et * 160 + 128] if fp == 0 else wx_bf[:, et * 160 + 128:et * 160 + 160]
                mm(b, bank[0:pr, :], lh, XC[:, et * T + ch * KC:et * T + ch * KC + KC],
                   et == 0, et == 3, inc=(sDB.h, 1) if et == 3 else None)
            sDB.n += 1
            dbv = (dbcb if fp == 0 else dbcb2)[ch % 2]
            b.wait("vector", sDB)
            if ch >= 2:
                b.wait("vector", sDBs[ch % 2])
            sDBe.n += 1
            b.on("vector", lambda e, o=dbv[0:pr, :], i=bank[0:pr, :], hh=sDBe.h:
                 e.tensor_copy(out=o, in_=i).then_inc(hh, 1))
            b.wait("gpsimd", sDBe)
            r0 = 0 if fp == 0 else 128
            b.dma("gpsimd", sDBs[ch % 2], DBCP[r0:r0 + pr, ch * KC:(ch + 1) * KC], dbv[0:pr, :])

    b.wait("gpsimd", sDBs[0])
    b.wait("gpsimd", sDBs[1])
    b.collective("AllReduce", ALU.add, DBCP[:, :], DBC[:, :], sARdbc)
    b.wait("gpsimd", sWoS[0])
    b.wait("gpsimd", sWoS[1])
    b.collective("AllGather", ALU.bypass, ag_wout_in[:, :], WOUTT[:, :], sAGwo2)

    arena.release()

    sBCBc = b.psem()
    sBCBl = b.psem()
    sBCBs = b.psem()
    bcst = xab[0][:, :].bitcast(F32)
    bcstb = xab[1]
    b.waitr("sync", sARdbc, 1)
    for hv in range(2):
        b.dma("sync", sBCBl, bcst[0:32, 0:T // 2], DBC[128:160, hv * (T // 2):(hv + 1) * (T // 2)])
        b.wait("vector", sBCBl)
        sBCBc.n += 1
        b.on("vector", lambda e, hh=sBCBc.h, hv=hv: e.tensor_copy(
            out=bcstb[0:32, hv * (T // 2):(hv + 1) * (T // 2)], in_=bcst[0:32, 0:T // 2]).then_inc(hh, 1))
        b.wait("sync", sBCBc)
    b.dma("sync", sBCBs, BCB[:, :], bcstb[0:32, 0:T])
    b.wait("sync", sBCBs)

    arena.mark()
    dtc = [arena.alloc(KC, F32) for _ in range(2)]
    dtb = [arena.alloc(KC, BF16) for _ in range(2)]
    Bb = arena.alloc(NS * KC, BF16)
    Cb = arena.alloc(NS * KC, BF16)
    HS = arena.alloc(NS * KC, BF16)
    delta_r = arena.alloc(4 * KC, F32)
    u_r = arena.alloc(4 * KC, BF16)
    tA = arena.alloc(KC, F32)
    tB = arena.alloc(KC, F32)
    dAb = [arena.alloc(KC, F32) for _ in range(2)]
    dxb = [arena.alloc(KC, BF16) for _ in range(2)]
    st_carry = arena.alloc(4 * NS, F32)
    yb = arena.alloc(KC, F32)
    szb = [arena.alloc(KC, BF16) for _ in range(2)]
    ygb = [arena.alloc(KC, BF16) for _ in range(2)]

    sDT = b.psem()
    sDel = b.psem()
    sSP = b.psem()
    sDelC = b.psem()
    sDA = b.psem()
    sSCN = b.psem()
    sZ = b.psem()
    sSil3 = b.psem()
    sYG = b.psem()
    sYGs = [b.sem("sYGs0"), b.sem("sYGs1")]
    sPRD = b.psem()
    sDtL = b.psem2()
    sBCl = b.psem()

    b.on("vector", lambda e: e.memset(st_carry, 0.0))

    for ch in range(NKC):
        r, c0 = ch // 2, (ch % 2) * KC
        xb = xab[ch % 2]
        if ch >= 2:
            b.waitr("sync", sZ, 4 * (ch - 1))
        for mt in range(NDT):
            b.dma("sync", sXa[ch % 2], xb[:, mt * KC:(mt + 1) * KC],
                  XG[r * D + mt * 128:r * D + (mt + 1) * 128, c0:c0 + KC])
        if ch >= 2:
            b.waitr("sync", sDel, 4 * (ch - 1))
        b.dma("sync", sDtL[ch % 2], dtc[ch % 2], DBC[0:128, ch * KC:(ch + 1) * KC])
        if ch >= 1:
            b.waitr("sync", sPRD, 4 * ch)
        for s in range(NS):
            b.dma("sync", sBCl, Bb[:, s * KC:(s + 1) * KC],
                  BCB[s:s + 1, ch * KC:(ch + 1) * KC].broadcast_to([128, KC]))
            b.dma("sync", sBCl, Cb[:, s * KC:(s + 1) * KC],
                  BCB[NS + s:NS + s + 1, ch * KC:(ch + 1) * KC].broadcast_to([128, KC]))
        b.wait("vector", sXa[ch % 2])
        b.wait("vector", sDtL[ch % 2])
        b.wait("vector", sBCl)
        if ch >= 2:
            b.waitr("vector", sDel, 4 * (ch - 1))
        sDT.n += 1
        b.on("vector", lambda e, o=dtb[ch % 2], i=dtc[ch % 2], hh=sDT.h:
             e.tensor_copy(out=o, in_=i).then_inc(hh, 1))
        b.wait("tensor", sDT)
        b.wait("tensor", sXa[ch % 2])
        for et in range(4):
            bank = ps[6 + et % 2]
            if 4 * ch + et >= 2:
                b.waitr("tensor", sSP, 2 * (4 * ch + et - 1))
            sDel.n += 1
            mm(b, bank[:, :], wdt_bf[:, et * 128:(et + 1) * 128], dtb[ch % 2],
               True, True, inc=(sDel.h, 1))
            b.wait("scalar", sDel)
            if 4 * ch + et >= 1:
                b.waitr("scalar", sDelC, 4 * ch + et)
            bcol = sb_bdt[:, et:et + 1]
            b.on("scalar", lambda e, o=tA, i=bank[:, :], bc=bcol:
                 e.activation(out=o, in_=i, func=AF.Abs, bias=bc))
            b.on("scalar", lambda e, o=tB, i=tA: e.activation(out=o, in_=i, func=AF.Exp, scale=-1.0))
            sSP.n += 1
            b.on("scalar", lambda e, o=tA, i=tB, hh=sSP.h:
                 e.activation(out=o, in_=i, func=AF.Ln, bias=1.0).then_inc(hh, 1))
            sSP.n += 1
            b.on("scalar", lambda e, o=tB, i=bank[:, :], bc=bcol, hh=sSP.h:
                 e.activation(out=o, in_=i, func=AF.Relu, bias=bc).then_inc(hh, 1))
            dl = delta_r[:, et * KC:(et + 1) * KC]
            b.wait("vector", sSP)
            sDelC.n += 1
            b.on("vector", lambda e, o=dl, i0=tA, i1=tB, hh=sDelC.h:
                 e.tensor_add(out=o, in0=i0, in1=i1).then_inc(hh, 1))
            b.wait("vector", sDelC)
            b.on("vector", lambda e, o=u_r[:, et * KC:(et + 1) * KC], i0=dl,
                 i1=XC[:, et * T + ch * KC:et * T + ch * KC + KC]:
                 e.tensor_mul(out=o, in0=i0, in1=i1))
        for et in range(4):
            dl = delta_r[:, et * KC:(et + 1) * KC]
            for s in range(NS):
                k = (ch * 4 + et) * NS + s
                da = dAb[k % 2]
                if s == 0:
                    b.waitr("scalar", sDelC, 4 * ch + et + 1)
                if k >= 2:
                    b.waitr("scalar", sSCN, k - 1)
                sDA.n += 1
                b.on("scalar", lambda e, o=da, i=dl, sc=A4[:, (et * NS + s):(et * NS + s) + 1], hh=sDA.h:
                     e.activation(out=o, in_=i, func=AF.Exp, scale=sc).then_inc(hh, 1))
                dx = dxb[k % 2]
                b.on("vector", lambda e, o=dx, i0=u_r[:, et * KC:(et + 1) * KC], i1=Bb[:, s * KC:(s + 1) * KC]:
                     e.tensor_mul(out=o, in0=i0, in1=i1))
                b.wait("vector", sDA)
                sSCN.n += 1
                b.on("vector", lambda e, o=HS[:, s * KC:(s + 1) * KC], d0=da, d1=dx,
                     ini=st_carry[:, et * NS + s:et * NS + s + 1], hh=sSCN.h:
                     e.tensor_tensor_scan(out=o, data0=d0, data1=d1, initial=ini,
                                          op0=ALU.mult, op1=ALU.add).then_inc(hh, 1))
            b.wait("vector", sSCN)
            b.on("vector", lambda e, o=st_carry[:, et * NS:(et + 1) * NS],
                 i=HS[:, KC - 1:NS * KC:KC]:
                 e.tensor_copy(out=o, in_=i))
            b.on("vector", lambda e: e.tensor_mul(out=HS[:, :], in0=HS[:, :], in1=Cb[:, :]))
            b.on("vector", lambda e, o=yb, i=HS[:, :].rearrange("p (s t) -> p t s", s=NS):
                 e.tensor_reduce(out=o, in_=i, axis=AX.X, op=ALU.add))
            b.on("vector", lambda e, o=yb, i=XC[:, et * T + ch * KC:et * T + ch * KC + KC],
                 s=sb_dp[:, et:et + 1]:
                 e.scalar_tensor_tensor(out=o, in0=i, scalar=s, in1=o, op0=ALU.mult, op1=ALU.add))
            zbank = ps[2 + et % 2]
            g = 4 * ch + et
            if g >= 2:
                b.waitr("tensor", sSil3, g - 1)
            for mt in range(NDT):
                mm(b, zbank[:, :], winz_bf[:, mt * KC + et * 128:mt * KC + (et + 1) * 128],
                   xb[:, mt * KC:(mt + 1) * KC], mt == 0, mt == NDT - 1,
                   inc=(sZ.h, 1) if mt == NDT - 1 else None)
            sZ.n += 1
            szv = szb[g % 2]
            b.wait("scalar", sZ)
            if g >= 2:
                b.waitr("scalar", sYG, g - 1)
            sSil3.n += 1
            b.on("scalar", lambda e, o=szv, i=zbank[:, :], hh=sSil3.h:
                 e.activation(out=o, in_=i, func=AF.Silu).then_inc(hh, 1))
            yg = ygb[g % 2]
            b.wait("vector", sSil3)
            if g >= 2:
                b.wait("vector", sYGs[g % 2])
            sYG.n += 1
            sPRD.n += 1
            b.on("vector", lambda e, o=yg, i0=yb, i1=szv, hh=sYG.h:
                 e.tensor_mul(out=o, in0=i0, in1=i1).then_inc(hh, 1))
            b.on("vector", lambda e, hh=sPRD.h: e.sem_inc(hh, 1))
            b.wait("gpsimd", sYG)
            b.dma("gpsimd", sYGs[g % 2],
                  ag_yg_in[(ch // 2) * EC + et * 128:(ch // 2) * EC + (et + 1) * 128,
                           (ch % 2) * KC:(ch % 2) * KC + KC], yg)

    b.wait("gpsimd", sYGs[0])
    b.wait("gpsimd", sYGs[1])
    b.collective("AllToAll", ALU.bypass, ag_yg_in[:, :], YGL[:, :], sA2Ayg)
    b.tap("dbc", DBC, 160, T, F32, gate_sem=sARdbc)
    b.tap("ygin", ag_yg_in, E4, TC, BF16, gate_sem=sYGs)
    b.tap("ygl", YGL, E4, TC, BF16, gate_sem=sA2Ayg)
    arena.release()
    arena.release()

    arena.mark()
    ygl_sb = arena.alloc(32 * TC, BF16)
    wob = [arena.alloc(32 * 128, BF16) for _ in range(2)]
    ln3 = LnCtx(b, C, "ln3")
    sPO3 = b.psem()
    sYgl = b.psem()
    sWbL = b.psem2()

    b.waitr("sync", sA2Ayg, 1)
    b.waitr("sync", sAGwo2, 1)
    for et in range(32):
        b.dma("sync", sYgl, ygl_sb[:, et * TC:(et + 1) * TC], YGL[et * 128:(et + 1) * 128, :])
    b.wait("tensor", sYgl)
    for tch in range(2):
        if tch >= 1:
            b.wait("tensor", ln3.sBcE)
        for dm in range(NDT):
            gi = tch * NDT + dm
            wb = wob[gi % 2]
            if gi >= 2:
                b.waitr("sync", sPO3, gi - 1)
            for et in range(32):
                b.dma("sync", sWbL[gi % 2], wb[:, et * 128:(et + 1) * 128],
                      WOUTT[(et * 16 + dm) * 128:(et * 16 + dm + 1) * 128, :])
            b.wait("tensor", sWbL[gi % 2])
            bank = ps[gi % 2]
            if gi >= 2:
                b.waitr("tensor", ln3.sAdd, gi - 1)
            for et in range(32):
                mm(b, bank[:, :], wb[:, et * 128:(et + 1) * 128],
                   ygl_sb[:, et * TC + tch * KC:et * TC + tch * KC + KC],
                   et == 0, et == 31, inc=(sPO3.h, 1) if et == 31 else None)
            sPO3.n += 1
            b.wait("vector", sPO3)
            ln3.add_tile(dm, bank[:, :], X2F, tch * KC)
        ln3.stats_norm_store([(X3F, tch * KC, F32), (X3B, tch * KC, BF16)])
    arena.release()
    C.update(X3F=X3F, X3B=X3B, sX3=ln3.sSt, sLn3Add=ln3.sAdd)
    b.tap("x3f", X3F, D, TC, F32, gate_sem=ln3.sSt[0])


def build_p4(b, C):
    b.pool_reset()
    """L1 fp4 FFN: dequant (d1) + AG, gate/up (silu*), down, RMS, +res, LN."""
    arena, ps = C["arena"], C["ps"]
    sb_eye, sb_consts = C["sb_eye"], C["sb_consts"]
    X3F, X3B = C["X3F"], C["X3B"]
    NT1 = I1 // 128
    GUPC = 2 * I1 // W // 128
    cHALF = sb_consts[:, 4:5]
    cM2 = sb_consts[:, 5:6]
    cM6 = sb_consts[:, 6:7]

    gup = b.din("gup", [GUPC * 128, 256], I32)
    gusT = b.din("gusT", [GUPC * 128, 16], F16)
    dnp = b.din("dnp", [256, I1 // 8], I32)
    dnsT = b.din("dnsT", [256, NT1], F16)

    ag_w1gu_in = b.dint("ag_w1gu_in", [GUPC * 16 * 128, 128], BF16)
    W1GUT = b.dint("W1GUT", [2 * NT1 * 16 * 128, 128], BF16)
    ag_w1d_in = b.dint("ag_w1d_in", [2 * NT1 * 128, 128], BF16)
    W1DT = b.dint("W1DT", [NDT * NT1 * 128, 128], BF16)
    OUT = b.dout("out", [D, TC], F32)
    sAGw1 = b.sem("sAGw1")

    arena.mark()
    pk = [arena.alloc(256, I32) for _ in range(2)]
    sc16 = [arena.alloc(16, F16) for _ in range(2)]
    scf = [arena.alloc(16, F32) for _ in range(2)]
    m2sc = [arena.alloc(16, F32) for _ in range(2)]
    nib = arena.alloc(2048, I32)
    fbv = arena.alloc(2048, F32)
    sfb = arena.alloc(2048, F32)
    l3b = arena.alloc(2048, F32)
    t1b = [arena.alloc(2048, F32) for _ in range(2)]
    t2b = [arena.alloc(2048, F32) for _ in range(2)]
    ub = arena.alloc(2048, F32)
    sgn = arena.alloc(2048, F32)
    wbf = [arena.alloc(2048, BF16) for _ in range(2)]
    wst = [arena.alloc(1024, BF16) for _ in range(2)]

    sPkL = b.psem2()
    sScL = b.psem2()
    sNib = b.psem()
    sL3 = b.psem()
    sT12 = b.psem()
    sUd = b.psem()
    sWbf = b.psem()
    sTrp = b.psem()
    sEvc = b.psem()
    sD1S = b.psem2()

    b.wait("tensor", C["sLn3Add"])
    unit = [0]
    bgrp = [0]

    def d1_unit(src_hbm, r0, c0i, nci, sc_hbm, sc_r0, sc_c0, ag_buf, row_of):
        k = unit[0]
        unit[0] += 1
        p = k % 2
        nv = nci * 8
        ng = nv // 128
        V = lambda f: b.on("vector", f)
        if k >= 2:
            b.waitr("sync", sNib, k - 1)
        b.dma("sync", sPkL[p], pk[p][:, 0:nci], src_hbm[r0:r0 + 128, c0i:c0i + nci])
        b.dma("sync", sScL[p], sc16[p][:, 0:ng], sc_hbm[sc_r0:sc_r0 + 128, sc_c0:sc_c0 + ng])
        b.wait("vector", sPkL[p])
        b.wait("vector", sScL[p])
        V(lambda e, o=scf[p][:, 0:ng], i=sc16[p][:, 0:ng]: e.tensor_copy(out=o, in_=i))
        for kk in range(8):
            ov = nib[:, kk:nv:8]
            iv = pk[p][:, 0:nci]
            if kk == 7:
                sNib.n += 1
                V(lambda e, o=ov, i=iv, s1=4 * kk, hh=sNib.h:
                  e.tensor_scalar(out=o, in0=i, scalar1=s1, scalar2=15,
                                  op0=ALU.logical_shift_right, op1=ALU.bitwise_and).then_inc(hh, 1))
            else:
                V(lambda e, o=ov, i=iv, s1=4 * kk:
                  e.tensor_scalar(out=o, in0=i, scalar1=s1, scalar2=15,
                                  op0=ALU.logical_shift_right, op1=ALU.bitwise_and))
        b.wait("vector", sNib)
        V(lambda e, o=fbv[:, 0:nv], i=nib[:, 0:nv]: e.tensor_copy(out=o, in_=i))
        V(lambda e, o=sfb[:, 0:nv], i=fbv[:, 0:nv]:
          e.tensor_single_scalar(out=o, in_=i, scalar=8.0, op=ALU.is_ge))
        sL3.n += 1
        V(lambda e, o=l3b[:, 0:nv], i0=sfb[:, 0:nv], i1=fbv[:, 0:nv], hh=sL3.h:
          e.scalar_tensor_tensor(out=o, in0=i0, scalar=-8.0, in1=i1,
                                 op0=ALU.mult, op1=ALU.add).then_inc(hh, 1))
        b.waitr("scalar", sL3, k + 1)
        if k >= 2:
            b.waitr("scalar", sUd, k - 1)
        sT12.n += 1
        b.on("scalar", lambda e, o=t1b[p][:, 0:nv], i=l3b[:, 0:nv]:
             e.activation(out=o, in_=i, func=AF.Relu, scale=cHALF, bias=cM2))
        b.on("scalar", lambda e, o=t2b[p][:, 0:nv], i=l3b[:, 0:nv], hh=sT12.h:
             e.activation(out=o, in_=i, func=AF.Relu, bias=cM6).then_inc(hh, 1))
        b.waitr("vector", sT12, k + 1)
        V(lambda e, o=ub[:, 0:nv], i0=l3b[:, 0:nv], i1=t1b[p][:, 0:nv]:
          e.scalar_tensor_tensor(out=o, in0=i0, scalar=0.5, in1=i1,
                                 op0=ALU.mult, op1=ALU.add))
        sUd.n += 1
        V(lambda e, o=ub[:, 0:nv], i0=ub[:, 0:nv], i1=t2b[p][:, 0:nv], hh=sUd.h:
          e.tensor_add(out=o, in0=i0, in1=i1).then_inc(hh, 1))
        V(lambda e, o=sgn[:, 0:nv], i=sfb[:, 0:nv]:
          e.tensor_scalar(out=o, in0=i, scalar1=-2.0, scalar2=1.0,
                          op0=ALU.mult, op1=ALU.add))
        for gg in range(ng):
            V(lambda e, o=sgn[:, gg * 128:(gg + 1) * 128],
              s=scf[p][:, gg:gg + 1]:
              e.tensor_single_scalar(out=o, in_=o, scalar=s, op=ALU.mult))
        if k >= 2:
            b.waitr("vector", sTrp, 2 * k - 2)
        sWbf.n += 1
        V(lambda e, o=wbf[p][:, 0:nv], i0=ub[:, 0:nv], i1=sgn[:, 0:nv], hh=sWbf.h:
          e.tensor_tensor(out=o, in0=i0, in1=i1, op=ALU.mult).then_inc(hh, 1))
        b.waitr("tensor", sWbf, k + 1)
        j0 = 0
        while j0 < ng:
            gcnt = min(8, ng - j0)
            g = bgrp[0]
            bgrp[0] += 1
            bank = ps[g % 4][:, :].bitcast(BF16)
            if g >= 4:
                b.waitr("tensor", sEvc, g - 3)
            for t in range(gcnt):
                fin = t == gcnt - 1
                if fin:
                    sTrp.n += 1
                def tr(e, o=bank[:, t * 128:(t + 1) * 128],
                       i=wbf[p][:, (j0 + t) * 128:(j0 + t + 1) * 128], f=fin, hh=sTrp.h):
                    ins = e.transpose(o, i, sb_eye)
                    if f:
                        ins.then_inc(hh, 1)
                b.on("tensor", tr)
            gp = g % 2
            b.waitr("vector", sTrp, g + 1)
            b.wait("vector", sD1S[gp])
            sEvc.n += 1
            V(lambda e, o=wst[gp][:, 0:gcnt * 128], i=bank[:, 0:gcnt * 128], hh=sEvc.h:
              e.tensor_copy(out=o, in_=i).then_inc(hh, 1))
            b.waitr("scalar", sEvc, g + 1)
            r0s = row_of(j0)
            for t in range(gcnt):
                b.dma("scalar", sD1S[gp],
                      ag_buf[r0s + t * 128:r0s + (t + 1) * 128, :],
                      wst[gp][:, t * 128:(t + 1) * 128])
            j0 += gcnt

    for rt in range(GUPC):
        d1_unit(gup, rt * 128, 0, 256, gusT, rt * 128, 0,
                ag_w1gu_in, lambda j0, rt=rt: (rt * 16 + j0) * 128)
    for gl in range(2):
        for cc, (c0i, nci) in enumerate(((0, 256), (256, 256), (512, 192))):
            d1_unit(dnp, gl * 128, c0i, nci, dnsT, gl * 128, cc * 16,
                    ag_w1d_in, lambda j0, gl=gl, cc=cc: (gl * NT1 + cc * 16 + j0) * 128)

    b.wait("gpsimd", sD1S[0])
    b.wait("gpsimd", sD1S[1])
    b.collective("AllGather", ALU.bypass, ag_w1gu_in[:, :], W1GUT[:, :], sAGw1)
    b.collective("AllGather", ALU.bypass, ag_w1d_in[:, :], W1DT[:, :], sAGw1)
    arena.release()
    b.tap("w1gut", W1GUT, 4096, 128, BF16, gate_sem=sAGw1)
    b.tap("w1dt", W1DT, 4096, 128, BF16, gate_sem=sAGw1)

    arena.mark()
    xb3_sb = arena.alloc(NDT * TC, BF16)
    H4 = arena.alloc(NT1 * TC, BF16)
    arena.mark()
    wbuf = [arena.alloc(32 * 128, BF16) for _ in range(2)]
    sg_buf = [arena.alloc(KC, BF16) for _ in range(2)]

    sP4g = b.psem()
    sSil = b.psem()
    sHE = b.psem()
    sDnG = b.psem()
    sX3bL = b.psem()
    sW1L = b.psem2()
    sWcL = b.psem2()

    b.wait("sync", sAGw1, 2)
    for mt in range(NDT):
        b.dma("sync", sX3bL, xb3_sb[:, mt * TC:(mt + 1) * TC], X3B[mt * 128:(mt + 1) * 128, :])
    b.wait("tensor", sX3bL)
    for it in range(NT1):
        wb = wbuf[it % 2]
        if it >= 2:
            b.waitr("sync", sP4g, 4 * (it - 1))
        for mt in range(NDT):
            row = (it * 16 + mt) * 128
            b.dma("sync", sW1L[it % 2], wb[:, mt * 128:(mt + 1) * 128], W1GUT[row:row + 128, :])
        for mt in range(NDT):
            row = ((NT1 + it) * 16 + mt) * 128
            b.dma("sync", sW1L[it % 2], wb[:, (16 + mt) * 128:(17 + mt) * 128], W1GUT[row:row + 128, :])
        b.wait("tensor", sW1L[it % 2])
        for tch in range(2):
            for gu in range(2):
                bank = ps[(it % 2) * 4 + tch * 2 + gu]
                if it >= 2:
                    if gu == 0:
                        b.waitr("tensor", sSil, 2 * (it - 1))
                    else:
                        b.waitr("tensor", sHE, 2 * (it - 1))
                for mt in range(NDT):
                    mm(b, bank[:, :], wb[:, (gu * 16 + mt) * 128:(gu * 16 + mt + 1) * 128],
                       xb3_sb[:, mt * TC + tch * KC:mt * TC + tch * KC + KC],
                       mt == 0, mt == NDT - 1, inc=(sP4g.h, 1) if mt == NDT - 1 else None)
                sP4g.n += 1
        for tch in range(2):
            gbank = ps[(it % 2) * 4 + tch * 2 + 0]
            ubank = ps[(it % 2) * 4 + tch * 2 + 1]
            sgb = sg_buf[tch]
            kk = 2 * it + tch
            if kk >= 2:
                b.waitr("scalar", sHE, kk - 1)
            b.waitr("scalar", sP4g, 4 * it + tch * 2 + 1)
            sSil.n += 1
            b.on("scalar", lambda e, o=sgb, i=gbank[:, :], hh=sSil.h:
                 e.activation(out=o, in_=i, func=AF.Silu).then_inc(hh, 1))
            b.wait("vector", sSil)
            b.waitr("vector", sP4g, 4 * it + tch * 2 + 2)
            sHE.n += 1
            b.on("vector", lambda e, o=H4[:, it * TC + tch * KC:it * TC + tch * KC + KC],
                 i0=sgb, i1=ubank[:, :], hh=sHE.h:
                 e.tensor_tensor(out=o, in0=i0, in1=i1, op=ALU.mult).then_inc(hh, 1))

    arena.release()
    wcol = [arena.alloc(NT1 * 128, BF16) for _ in range(2)]
    ln4 = LnCtx(b, C, "ln4")
    b.wait("tensor", sHE)
    b.wait("sync", sP4g)
    b.wait("sync", sHE)
    b.wait("scalar", sP4g)
    b.wait("scalar", sHE)
    for tch in range(2):
        if tch >= 1:
            b.wait("tensor", ln4.sBcE)
        for g in range(2):
            if tch + g > 0:
                b.wait("tensor", ln4.sAdd)
            for dml in range(8):
                dm = g * 8 + dml
                gidx = (tch * 2 + g) * 8 + dml
                wc = wcol[gidx % 2]
                q = "sync" if gidx % 2 == 0 else "scalar"
                qs = sWcL[gidx % 2]
                if gidx >= 2:
                    b.waitr(q, sDnG, gidx - 1)
                for i in range(NT1):
                    b.dma(q, qs, wc[:, i * 128:(i + 1) * 128],
                          W1DT[(dm * NT1 + i) * 128:(dm * NT1 + i + 1) * 128, :])
                b.wait("tensor", qs)
                for i in range(NT1):
                    mm(b, ps[dml][:, :], wc[:, i * 128:(i + 1) * 128],
                       H4[:, i * TC + tch * KC:i * TC + tch * KC + KC],
                       i == 0, i == NT1 - 1, inc=(sDnG.h, 1) if i == NT1 - 1 else None)
                sDnG.n += 1
            b.wait("vector", sDnG)
            for dml in range(8):
                dm = g * 8 + dml
                ln4.sAdd.n += 1
                b.on("vector", lambda e, o=ln4.xpre[:, dm * KC:(dm + 1) * KC],
                     i=ps[dml][:, :], hh=ln4.sAdd.h:
                     e.tensor_copy(out=o, in_=i).then_inc(hh, 1))
        ln4.rms_scale_inplace()
        for dm in range(NDT):
            ln4.add_tile(dm, ln4.xpre[:, dm * KC:(dm + 1) * KC], X3F, tch * KC)
        ln4.stats_norm_store([(OUT, tch * KC, F32)])
    arena.release()


def make_mask_t():
    k = np.arange(KC)[:, None]
    out = np.zeros((KC, 4 * 128), dtype=np.float32)
    for v in range(4):
        q = np.arange(128)[None, :]
        out[:, v * 128:(v + 1) * 128] = np.where(k <= v * 128 + q, 0.0, np.float32(MASKV))
    return out


def host_inputs(inputs, c, phases):
    x = np.asarray(inputs["L0_attn_hidden_states"])[0]
    d = {
        "xtf": np.ascontiguousarray(x[c * TC:(c + 1) * TC, :].T),
        "onesb": np.ones((128, 1), dtype=bfdt),
        "eyeb": np.eye(128, dtype=bfdt),
        "ones1f": np.ones((1, 128), dtype=np.float32),
        "consts": np.tile(np.array([[EPS_LN, EPS_RMS, -0.5, QSC, 0.5, -2.0, -6.0, 1.0]],
                                   dtype=np.float32), (128, 1)),
    }
    if "p1" in phases:
        d.update({
            "wq": np.ascontiguousarray(np.asarray(inputs["L0_attn_q_wei"])[:, c * 256:(c + 1) * 256]),
            "wk": np.ascontiguousarray(np.asarray(inputs["L0_attn_k_wei"])[:, c * 256:(c + 1) * 256]),
            "wv": np.ascontiguousarray(np.asarray(inputs["L0_attn_v_wei"])[:, c * 256:(c + 1) * 256]),
            "wo": np.asarray(inputs["L0_attn_out_wei"]),
            "maskt": make_mask_t(),
        })
    if "p2" in phases:
        d.update({
            "g_idxT": np.ascontiguousarray(np.asarray(inputs["L0_ffn_gate_idx"])[:, c * 256:(c + 1) * 256].T),
            "u_idxT": np.ascontiguousarray(np.asarray(inputs["L0_ffn_up_idx"])[:, c * 256:(c + 1) * 256].T),
            "d_idxT": np.ascontiguousarray(np.asarray(inputs["L0_ffn_down_idx"])[:, c * 688:(c + 1) * 688].T),
            "lutb": np.tile(np.asarray(inputs["L0_ffn_lut"])[None, :].astype(np.float32), (128, 1)),
        })
    if "p3" in phases:
        W_in = np.asarray(inputs["L1_attn_W_in"])
        sl = slice(c * EC, (c + 1) * EC)

        def v4(a):
            return np.ascontiguousarray(a[sl].reshape(4, 128).T)
        d.update({
            "winxT": np.ascontiguousarray(W_in[sl, :].T),
            "winzT": np.ascontiguousarray(W_in[E4 + c * EC:E4 + (c + 1) * EC, :].T),
            "convw4": np.ascontiguousarray(
                np.asarray(inputs["L1_attn_conv_w"])[sl].reshape(4, 128, 4).transpose(1, 0, 2).reshape(128, 16)),
            "convb4": v4(np.asarray(inputs["L1_attn_conv_b"])),
            "wxT": np.ascontiguousarray(np.asarray(inputs["L1_attn_W_x"])[:, sl].T),
            "wdtT": np.ascontiguousarray(np.asarray(inputs["L1_attn_W_dt"])[sl, :].T),
            "bdt4": v4(np.asarray(inputs["L1_attn_b_dt"])),
            "alog4": np.ascontiguousarray(
                np.asarray(inputs["L1_attn_A_log"])[sl].reshape(4, 128, NS).transpose(1, 0, 2).reshape(128, 4 * NS)),
            "dp4": v4(np.asarray(inputs["L1_attn_Dp"])),
            "woutT_t": np.ascontiguousarray(
                np.asarray(inputs["L1_attn_W_out"])[:, sl].T.reshape(4, 128, 16, 128)
                .transpose(0, 2, 1, 3).reshape(EC * 16, 128)),
        })
    if "p4" in phases:
        gupr = 2 * I1 // W
        d.update({
            "gup": np.ascontiguousarray(
                np.asarray(inputs["L1_ffn_gate_up_packed"])[c * gupr:(c + 1) * gupr, :]),
            "gusT": np.ascontiguousarray(
                np.asarray(inputs["L1_ffn_gate_up_scales"])[:, c * gupr:(c + 1) * gupr].T),
            "dnp": np.ascontiguousarray(
                np.asarray(inputs["L1_ffn_down_packed"])[c * 256:(c + 1) * 256, :]),
            "dnsT": np.ascontiguousarray(
                np.asarray(inputs["L1_ffn_down_scales"])[:, c * 256:(c + 1) * 256].T),
        })
    return d


ALL_PHASES = ("p1", "p2", "p3", "p4")


def kernel(**inputs):
    bb = build(phases=ALL_PHASES, taps=())
    in_maps = [host_inputs(inputs, c, set(ALL_PHASES)) for c in range(W)]
    res = run_bass_kernel_spmd(bb.nc, in_maps, core_ids=list(range(W)))
    parts = [np.asarray(res.results[c]["out"]).T for c in range(W)]
    return np.ascontiguousarray(
        np.concatenate(parts, axis=0).reshape(1, T, D)).astype(np.float32, copy=False)


def run_phases(inputs, phases=("p1",), taps=()):
    bb = build(phases=phases, taps=taps)
    in_maps = [host_inputs(inputs, c, set(phases)) for c in range(W)]
    res = run_bass_kernel_spmd(bb.nc, in_maps, core_ids=list(range(W)))
    return res.results


# version 03
# speedup vs baseline: 1.0588x; 1.0588x over previous
import os
os.environ.setdefault("TRN_ALLOW_ICI_ALL_TO_ALL", "1")

import numpy as np
import ml_dtypes
from contextlib import ExitStack
import concourse.bass as bass
import concourse.mybir as mybir
from concourse.bass_utils import run_bass_kernel_spmd

F32 = mybir.dt.float32
BF16 = mybir.dt.bfloat16
I32 = mybir.dt.int32
F16 = mybir.dt.float16
U8 = mybir.dt.uint8
AF = mybir.ActivationFunctionType
ALU = mybir.AluOpType
AX = mybir.AxisListType
bfdt = ml_dtypes.bfloat16

DTSIZE = {F32: 4, BF16: 2, I32: 4, F16: 2, U8: 1}

W = 8
T = 8192
TC = T // W
D = 2048
NDT = D // 128
HPC = 2
DH = 128
KC = 512
NKC = T // KC
NQT = T // 128
QSC = 1.0 / float(np.sqrt(128.0))
MASKV = -66504.0 * QSC
EPS_LN = 1e-5
EPS_RMS = 1e-6
I0 = 5504
I1 = 5632
E4 = 4096
EC = 512
NET = 4
NS = 16
RDT = 128


class Sem:
    def __init__(self, st, nc, name):
        self.h = st.enter_context(nc.semaphore(name))
        self.n = 0
        self.base = 0


class Arena:
    def __init__(self, st, nc, name, nbytes):
        self.t = st.enter_context(nc.sbuf_tensor(name, [128, nbytes], U8))
        self.nbytes = nbytes
        self.off = 0
        self.marks = []
        self.hi = 0

    def alloc(self, n_elems, dt):
        sz = n_elems * DTSIZE[dt]
        self.off = (self.off + 63) & ~63
        assert self.off + sz <= self.nbytes, f"arena overflow: {self.off + sz} > {self.nbytes}"
        v = self.t[:, self.off:self.off + sz].bitcast(dt)
        self.off += sz
        self.hi = max(self.hi, self.off)
        return v

    def mark(self):
        self.marks.append(self.off)

    def release(self):
        self.off = self.marks.pop()


class Builder:
    def __init__(self, phases, taps=(), detect_races=True):
        self.phases = set(phases)
        self.taps = set(taps)
        self.nc = bass.Bass("TRN2", target_bir_lowering=False,
                            detect_race_conditions=detect_races)
        self.q = {"sync": [], "vector": [], "scalar": [], "tensor": [], "gpsimd": []}
        self.st = ExitStack()
        self.dram = {}
        self.outputs = []

    def on(self, eng, fn):
        self.q[eng].append(fn)

    def wait(self, eng, sem, val=None):
        v = sem.n if val is None else val
        if v > 0:
            self.on(eng, lambda e, h=sem.h, v=v: e.wait_ge(h, v))

    def waitr(self, eng, sem, rel):
        v = sem.base + rel
        if v > 0 and rel > 0:
            self.on(eng, lambda e, h=sem.h, v=v: e.wait_ge(h, v))

    def sem(self, name):
        return Sem(self.st, self.nc, name)

    def pool_reset(self):
        self._pool_i = 0

    def psem(self):
        if not hasattr(self, "_pool"):
            self._pool = []
            self._pool_i = 0
        if self._pool_i == len(self._pool):
            self._pool.append(Sem(self.st, self.nc, f"pool{self._pool_i}"))
        s = self._pool[self._pool_i]
        s.base = s.n
        self._pool_i += 1
        return s

    def psem2(self):
        return [self.psem(), self.psem()]

    def din(self, name, shape, dt):
        ap = self.nc.dram_tensor(name, list(shape), dt, kind="ExternalInput").ap()
        self.dram[name] = ap
        return ap

    def dint(self, name, shape, dt):
        ap = self.nc.dram_tensor(name, list(shape), dt).ap()
        self.dram[name] = ap
        return ap

    def dout(self, name, shape, dt):
        ap = self.nc.dram_tensor(name, list(shape), dt, kind="ExternalOutput").ap()
        self.dram[name] = ap
        self.outputs.append(name)
        return ap

    def dma(self, eng, sem, out, in_):
        sem.n += 16
        self.on(eng, lambda e, o=out, i=in_, h=sem.h: e.dma_start(out=o, in_=i).then_inc(h, 16))
        return sem.n

    def sem2(self, name):
        return [Sem(self.st, self.nc, name + "a"), Sem(self.st, self.nc, name + "b")]

    def collective(self, kind, op, in_ap, out_ap, sem):
        sem.n += 1
        def f(e, k=kind, o=op, i=in_ap, u=out_ap, h=sem.h):
            e.collective_compute(k, o, replica_groups=[list(range(W))],
                                 ins=[i.opt()], outs=[u.opt()]).then_inc(h, 1)
        self.on("gpsimd", f)

    def tap(self, name, hbm_ap, rows, cols, dt, gate_sem=None):
        if name not in self.taps:
            return
        o = self.dout("tap_" + name, [rows, cols], dt)
        s = self.sem("sTap_" + name)
        if gate_sem is not None:
            self.wait("sync", gate_sem)
        r = 0
        while r < rows:
            pr = min(128, rows - r)
            self.dma("sync", s, o[r:r + pr, :], hbm_ap[r:r + pr, :])
            r += pr
        self.wait("sync", s)

    def finish(self):
        nc = self.nc
        with nc.Block() as block:
            @block.sync
            def _(e):
                for f in self.q["sync"]:
                    f(e)

            @block.vector
            def _(e):
                for f in self.q["vector"]:
                    f(e)

            @block.scalar
            def _(e):
                for f in self.q["scalar"]:
                    f(e)

            @block.tensor
            def _(e):
                for f in self.q["tensor"]:
                    f(e)

            @block.gpsimd
            def _(e):
                for f in self.q["gpsimd"]:
                    f(e)
        self.st.close()
        return nc


def mm(b, psum, lhsT, rhs, start, stop, inc=None):
    def f(e, p=psum, l=lhsT, r=rhs, s0=start, s1=stop, i=inc):
        ins = e.matmul(p, l, r, start=s0, stop=s1)
        if i is not None:
            ins.then_inc(i[0], i[1])
    b.on("tensor", f)


def build(phases=("p1",), taps=(), detect_races=True):
    b = Builder(phases, taps, detect_races)
    nc = b.nc
    st = b.st

    xtf = b.din("xtf", [D, TC], F32)
    onesb = b.din("onesb", [128, 1], BF16)
    eyeb = b.din("eyeb", [128, 128], BF16)
    ones1f = b.din("ones1f", [1, 128], F32)
    consts = b.din("consts", [128, 8], F32)

    arena = Arena(st, nc, "arena", 204 * 1024)
    ps = [st.enter_context(nc.psum_tensor(f"ps{i}", [128, 512], F32)) for i in range(8)]

    sCst = b.sem("sCst")
    sGP = b.sem("sGP")

    sb_ones = arena.alloc(1, BF16)
    sb_eye = arena.alloc(128, BF16)
    sb_ones1f = arena.alloc(128, F32)
    sb_consts = arena.alloc(8, F32)
    b.dma("sync", sCst, sb_ones, onesb[:, :])
    b.dma("sync", sCst, sb_eye, eyeb[:, :])
    b.dma("sync", sCst, sb_ones1f[0:1, :], ones1f[:, :])
    b.dma("sync", sCst, sb_consts, consts[:, :])

    C = dict(arena=arena, ps=ps, sCst=sCst, sGP=sGP, sb_ones=sb_ones,
             sb_eye=sb_eye, sb_ones1f=sb_ones1f, sb_consts=sb_consts, xtf=xtf)

    sAGxt = b.sem("sAGxt")
    ag_xt_in = b.dint("ag_xt_in", [D, TC], BF16)
    XTG = b.dint("XTG", [W * D, TC], BF16)

    arena.mark()
    xt_f32 = arena.alloc(NDT * TC, F32)
    xt_bf = arena.alloc(NDT * TC, BF16)
    sCast = b.sem("sCast")
    sP0L = b.sem("sP0L")
    sP0S = b.sem("sP0S")
    for t in range(NDT):
        b.dma("sync", sP0L, xt_f32[:, t * TC:(t + 1) * TC], xtf[t * 128:(t + 1) * 128, :])
    b.wait("vector", sP0L)
    sCast.n += 1
    b.on("vector", lambda e, o=xt_bf, i=xt_f32, h=sCast.h: e.tensor_copy(out=o, in_=i).then_inc(h, 1))
    b.wait("sync", sCast)
    for t in range(NDT):
        b.dma("sync", sP0S, ag_xt_in[t * 128:(t + 1) * 128, :], xt_bf[:, t * TC:(t + 1) * TC])
    b.wait("gpsimd", sP0S)
    b.collective("AllGather", ALU.bypass, ag_xt_in[:, :], XTG[:, :], sAGxt)
    arena.release()
    C.update(sAGxt=sAGxt, XTG=XTG, sP0S=sP0S)

    b.tap("xtg", XTG, W * D, TC, BF16, gate_sem=sAGxt)

    if "p1" in b.phases:
        build_p1(b, C)
    if "p2" in b.phases:
        build_d0(b, C)
        build_p2(b, C)
    if "p3" in b.phases:
        build_p3(b, C)
    if "p4" in b.phases:
        build_p4(b, C)

    b.finish()
    return b


def build_p1(b, C):
    b.pool_reset()
    arena, ps = C["arena"], C["ps"]
    sCst, sGP = C["sCst"], C["sGP"]
    sb_ones, sb_eye, sb_ones1f = C["sb_ones"], C["sb_eye"], C["sb_ones1f"]
    sb_consts = C["sb_consts"]
    XTG, sAGxt, xtf = C["XTG"], C["sAGxt"], C["xtf"]
    cEPSLN = sb_consts[0:1, 0:1]
    cNHALF = sb_consts[0:1, 2:3]
    cQSC = sb_consts[:, 3:4]

    wq = b.din("wq", [D, HPC * DH], F32)
    wk = b.din("wk", [D, HPC * DH], F32)
    wv = b.din("wv", [D, HPC * DH], F32)
    wo = b.din("wo", [D, D], F32)
    maskt = b.din("maskt", [KC, 4 * 128], BF16)

    att_a2a_in = b.dint("att_a2a_in", [D, TC], BF16)
    ATTL = b.dint("ATTL", [D, TC], BF16)
    X1F = b.dint("X1F", [D, TC], F32)
    X1B = b.dint("X1B", [D, TC], BF16)
    bc1c = b.dint("p1c_bcd", [1, 2 * KC], F32)
    sA2Aatt = b.sem("sA2Aatt")

    arena.mark()
    xg_buf = [arena.alloc(NDT * KC, BF16) for _ in range(2)]
    w_stage = xg_buf[0][:, :].bitcast(F32)
    wq_bf = arena.alloc(NDT * HPC * DH, BF16)
    wk_bf = arena.alloc(NDT * HPC * DH, BF16)
    wv_bf = arena.alloc(NDT * HPC * DH, BF16)
    sWc = b.psem()
    sWL = b.psem()
    b.wait("sync", C["sP0S"])
    for src, dstv in ((wq, wq_bf), (wk, wk_bf), (wv, wv_bf)):
        for t in range(NDT):
            b.dma("sync", sWL, w_stage[:, t * 256:(t + 1) * 256], src[t * 128:(t + 1) * 128, :])
        b.wait("vector", sWL)
        sWc.n += 1
        b.on("vector", lambda e, s=w_stage, d=dstv, h=sWc.h: e.tensor_copy(out=d, in_=s).then_inc(h, 1))
        b.wait("sync", sWc)

    QT = [arena.alloc(T, BF16) for _ in range(HPC)]
    KT = [arena.alloc(T, BF16) for _ in range(HPC)]
    VSB = arena.alloc(64 * 256, BF16)
    sb_mask = arena.alloc(4 * KC, BF16)
    for kt in range(4):
        b.dma("sync", sCst, sb_mask[:, kt * KC:(kt + 1) * KC], maskt[kt * 128:(kt + 1) * 128, :])
    rows_bf = arena.alloc(2 * T, BF16)
    rows_s = xg_buf[1]

    sG = b.psem()
    sEvP = [b.sem("sEv0"), b.sem("sEv1")]
    nuse = [0, 0]
    b.wait("sync", sAGxt, 1)
    b.wait("tensor", sAGxt, 1)
    b.waitr("tensor", sWc, 3)
    b.wait("tensor", sCst)
    b.wait("vector", sCst)

    sXg = b.psem2()
    g_idx = 0

    def qkv_group(kind, idx, ch, buf):
        nonlocal g_idx
        g = g_idx
        p = g % 2
        bank = ps[p]
        if nuse[p] >= 1:
            b.waitr("tensor", sEvP[p], nuse[p])
        if kind in ("q", "k"):
            h = idx
            wsrc = wq_bf if kind == "q" else wk_bf
            for mt in range(NDT):
                mm(b, bank[:, :], wsrc[:, mt * 256 + h * 128:mt * 256 + (h + 1) * 128],
                   buf[:, mt * KC:(mt + 1) * KC], mt == 0, mt == NDT - 1,
                   inc=(sG.h, 1) if mt == NDT - 1 else None)
            sG.n += 1
            dst = (QT if kind == "q" else KT)[h]
            out_ap = dst[:, ch * KC:(ch + 1) * KC]
            src_ap = bank[:, :]
            scale = QSC if kind == "q" else None
        else:
            st4 = idx
            for mt in range(NDT):
                mm(b, bank[:, 0:256], buf[:, mt * KC + st4 * 128:mt * KC + (st4 + 1) * 128],
                   wv_bf[:, mt * 256:(mt + 1) * 256], mt == 0, mt == NDT - 1,
                   inc=(sG.h, 1) if mt == NDT - 1 else None)
            sG.n += 1
            stg = ch * 4 + st4
            out_ap = VSB[:, stg * 256:(stg + 1) * 256]
            src_ap = bank[:, 0:256]
            scale = None
        eng = "scalar" if p == 0 else "vector"
        b.wait(eng, sG)
        sEvP[p].n += 1
        if eng == "scalar":
            if scale is not None:
                b.on("scalar", lambda e, o=out_ap, i=src_ap, h2=sEvP[p].h:
                     e.mul(out=o, in_=i, mul=cQSC).then_inc(h2, 1))
            else:
                b.on("scalar", lambda e, o=out_ap, i=src_ap, h2=sEvP[p].h:
                     e.copy(out=o, in_=i).then_inc(h2, 1))
        else:
            if scale is not None:
                b.on("vector", lambda e, o=out_ap, i=src_ap, h2=sEvP[p].h, s=scale:
                     e.tensor_scalar_mul(out=o, in0=i, scalar1=s).then_inc(h2, 1))
            else:
                b.on("vector", lambda e, o=out_ap, i=src_ap, h2=sEvP[p].h:
                     e.tensor_copy(out=o, in_=i).then_inc(h2, 1))
        nuse[p] += 1
        g_idx += 1

    for ch in range(NKC):
        buf = xg_buf[ch % 2]
        r, c0 = ch // 2, (ch % 2) * KC
        if ch >= 2:
            b.waitr("sync", sG, 8 * (ch - 1))
        elif ch == 0:
            b.waitr("sync", sWc, 3)
        for mt in range(NDT):
            b.dma("sync", sXg[ch % 2], buf[:, mt * KC:(mt + 1) * KC],
                  XTG[r * D + mt * 128:r * D + (mt + 1) * 128, c0:c0 + KC])
        b.wait("tensor", sXg[ch % 2])
        for h in range(HPC):
            qkv_group("q", h, ch, buf)
            qkv_group("k", h, ch, buf)
        for st4 in range(4):
            qkv_group("v", st4, ch, buf)

    qt2_bf = xg_buf[0]
    pS = [ps[0], ps[1]]
    pAcc = ps[2]
    pL = ps[3]
    pRow = [ps[4], ps[5]]

    sSq = b.psem()
    sQn2m = b.psem()
    sRowE = b.psem()
    sKmax = b.psem()
    sSqr = b.psem()
    sPad = b.psem()
    sPrep = b.psem()
    sS = b.psem()
    sM = b.psem()
    sExp = b.psem()
    sAV = b.psem()
    sLi = b.psem()
    sAT = b.psem()
    sBcS = b.psem()
    sBcE = b.psem()

    pt_sb = [arena.alloc(4 * KC, BF16) for _ in range(2)]
    at_bf = [arena.alloc(KC, BF16) for _ in range(2)]
    linv_b = arena.alloc(KC, F32)
    bc1b = b.dint("p1b_bcd", [1, KC], F32)
    sShD = [b.sem("sShD0"), b.sem("sShD1")]

    cg = 0
    qtg = 0
    for h in range(HPC):
        if h == 0:
            b.wait("scalar", sG)
            b.wait("scalar", sEvP[0])
            b.wait("scalar", sEvP[1])
        else:
            b.wait("scalar", sS)
            b.wait("scalar", sQn2m)
        sSq.n += 1
        b.on("scalar", lambda e, o=qt2_bf, i=QT[h], hh=sSq.h: e.activation(out=o, in_=i, func=AF.Square).then_inc(hh, 1))
        b.wait("tensor", sSq)
        for part, rowbase in ((0, 0), (1, 64)):
            for chn in range(NKC):
                bank = pRow[chn % 2]
                if part * NKC + chn >= 2:
                    b.wait("tensor", sRowE, sRowE.n - 1)
                sQn2m.n += 1
                mm(b, bank[0:1, :], sb_ones, qt2_bf[:, chn * KC:(chn + 1) * KC],
                   True, True, inc=(sQn2m.h, 1))
                b.wait("scalar", sQn2m)
                sRowE.n += 1
                b.on("scalar", lambda e, o=rows_s[rowbase:rowbase + 1, chn * KC:(chn + 1) * KC],
                     i=bank[0:1, :], hh=sRowE.h: e.copy(out=o, in_=i).then_inc(hh, 1))
            if part == 0:
                b.wait("scalar", sQn2m)
                sSq.n += 1
                b.on("scalar", lambda e, o=qt2_bf, i=KT[h], hh=sSq.h: e.activation(out=o, in_=i, func=AF.Square).then_inc(hh, 1))
                b.wait("tensor", sSq)
        sSqr.n += 1
        b.on("scalar", lambda e, hh=sSqr.h: e.activation(out=rows_s[32:33, 0:T], in_=rows_s[0:1, 0:T], func=AF.Sqrt).then_inc(hh, 1))
        b.wait("vector", sSqr)
        b.on("vector", lambda e: e.tensor_scalar(out=rows_bf[0:1, 0:T], in0=rows_s[32:33, 0:T],
                                                 scalar1=-1.01, scalar2=-0.5, op0=ALU.mult, op1=ALU.add))
        b.wait("vector", sRowE)
        sKmax.n += 1
        b.on("vector", lambda e, hh=sKmax.h: e.tensor_reduce(
            out=rows_bf[32:33, 0:NKC], in_=rows_s[64:65, 0:T].rearrange("p (c k) -> p c k", c=NKC),
            axis=AX.X, op=ALU.max).then_inc(hh, 1))
        b.wait("scalar", sKmax)
        sSqr.n += 1
        b.on("scalar", lambda e, hh=sSqr.h: e.activation(out=rows_bf[32:33, NKC:2 * NKC], in_=rows_bf[32:33, 0:NKC], func=AF.Sqrt).then_inc(hh, 1))
        b.wait("vector", sSqr)
        sPad.n += 1
        b.on("vector", lambda e, hh=sPad.h: e.tensor_scalar(out=rows_bf[32:33, 2 * NKC:3 * NKC], in0=rows_bf[32:33, NKC:2 * NKC],
                                                 scalar1=1.01, scalar2=0.5, op0=ALU.mult, op1=ALU.add).then_inc(hh, 1))
        b.wait("vector", sPad)
        sPrep.n += 1
        b.on("vector", lambda e, hh=sPrep.h: e.tensor_copy(
            out=rows_bf[0:1, T:2 * T].rearrange("p (c k) -> p c k", c=NKC),
            in_=rows_bf[32:33, 2 * NKC:3 * NKC].rearrange("p (c o) -> p c o", o=1).broadcast_to([1, NKC, KC])).then_inc(hh, 1))
        b.wait("tensor", sPrep)

        def emit_S(qq, ci, my_cg):
            diag = (ci == qq)
            for j in range(4):
                kt = ci * 4 + j
                myjg = 4 * my_cg + j
                bank = pS[myjg % 2]
                if myjg >= 2:
                    b.waitr("tensor", sExp, myjg - 1)
                mm(b, bank[:, :],
                   rows_bf[0:1, T + kt * 128:T + (kt + 1) * 128],
                   rows_bf[0:1, qq * 512:(qq + 1) * 512],
                   True, False)
                mm(b, bank[:, :],
                   KT[h][:, kt * 128:(kt + 1) * 128], QT[h][:, qq * 512:(qq + 1) * 512],
                   False, True, inc=(sS.h, 1))
                sS.n += 1
                if diag:
                    b.waitr("vector", sS, myjg + 1)
                    sM.n += 1
                    b.on("vector", lambda e, pk=bank, jj=j, hh=sM.h: e.tensor_tensor(
                        out=pk[:, :], in0=pk[:, :], in1=sb_mask[:, jj * KC:(jj + 1) * KC],
                        op=ALU.add).then_inc(hh, 1))
                    b.wait("scalar", sM)
                else:
                    b.waitr("scalar", sS, myjg + 1)
                if j == 0 and my_cg >= 2:
                    b.waitr("scalar", sAV, my_cg - 1)
                sExp.n += 1
                b.on("scalar", lambda e, pk=bank,
                     o=pt_sb[my_cg % 2][:, j * KC:(j + 1) * KC], hh=sExp.h:
                     e.activation(out=o, in_=pk[:, :], func=AF.Exp).then_inc(hh, 1))

        def emit_AV(qq, ci, my_cg, nch, my_qtg):
            b.waitr("tensor", sExp, 4 * (my_cg + 1))
            if ci == 0:
                b.waitr("tensor", sAT, my_qtg)
                b.waitr("tensor", sLi, my_qtg)
            pt = pt_sb[my_cg % 2]
            first = (ci == 0)
            last = (ci == nch - 1)
            for j in range(4):
                kt = ci * 4 + j
                mm(b, pAcc[:, :], VSB[:, kt * 256 + h * 128:kt * 256 + (h + 1) * 128],
                   pt[:, j * KC:(j + 1) * KC], first and j == 0, last and j == 3)
            for j in range(4):
                mm(b, pL[0:1, :], sb_ones, pt[:, j * KC:(j + 1) * KC],
                   first and j == 0, last and j == 3,
                   inc=(sAV.h, 1) if j == 3 else None)
            sAV.n += 1

        def emit_tail(qq, my_qtg):
            b.wait("vector", sAV)
            b.waitr("vector", sAT, my_qtg)
            sLi.n += 1
            b.on("vector", lambda e, hh=sLi.h:
                 e.reciprocal(out=linv_b[0:1, :], in_=pL[0:1, :]).then_inc(hh, 1))
            b.wait("sync", sLi)
            b.waitr("sync", sAT, my_qtg)
            b.dma("sync", sBcS, bc1b[0:1, :], linv_b[0:1, :])
            b.wait("sync", sBcS)
            b.dma("sync", sBcE, linv_b, bc1b[0:1, :].broadcast_to([128, KC]))
            b.wait("vector", sBcE)
            if my_qtg >= 2:
                b.wait("vector", sShD[my_qtg % 2])
            sAT.n += 1
            b.on("vector", lambda e, o=at_bf[my_qtg % 2], hh=sAT.h: e.tensor_tensor(
                 out=o, in0=pAcc[:, :], in1=linv_b, op=ALU.mult).then_inc(hh, 1))
            b.wait("gpsimd", sAT)
            jc = (qq * 4) // 8
            col = (qq % 2) * 512
            b.dma("gpsimd", sShD[my_qtg % 2],
                  att_a2a_in[jc * 256 + h * 128:jc * 256 + (h + 1) * 128, col:col + KC],
                  at_bf[my_qtg % 2])

        NQQ = NQT // 4
        for qq in range(NQQ):
            nch = qq + 1
            cgs = []
            for ci in range(nch):
                emit_S(qq, ci, cg)
                cgs.append(cg)
                cg += 1
                if ci >= 1:
                    emit_AV(qq, ci - 1, cgs[ci - 1], nch, qtg)
            emit_AV(qq, nch - 1, cgs[nch - 1], nch, qtg)
            emit_tail(qq, qtg)
            qtg += 1

    b.wait("gpsimd", sShD[0])
    b.wait("gpsimd", sShD[1])
    b.collective("AllToAll", ALU.bypass, att_a2a_in[:, :], ATTL[:, :], sA2Aatt)

    b.tap("attl", ATTL, D, TC, BF16, gate_sem=sA2Aatt)

    arena.release()
    arena.mark()
    wo_stage = [arena.alloc(D, F32) for _ in range(2)]
    wo_bf = arena.alloc(NDT * D, BF16)
    attl_sb = arena.alloc(NDT * TC, BF16)
    x1pre = arena.alloc(NDT * KC, F32)
    xres_buf = [arena.alloc(KC, F32) for _ in range(2)]
    sq_buf = [arena.alloc(KC, BF16) for _ in range(2)]
    xb_buf = [arena.alloc(KC, BF16) for _ in range(2)]
    rows2 = arena.alloc(8 * KC, F32)
    rstd_b = arena.alloc(KC, F32)
    mrow_b = arena.alloc(KC, F32)

    def R2(slot):
        return rows2[0:1, slot * KC:(slot + 1) * KC]

    sWoC = b.psem()
    sWoL = b.psem2()
    sAtl = b.psem()
    sXr = b.psem2()
    sStp = b.sem2("sStp")
    sPO = b.psem()
    sX1p = b.psem()
    sSqE = b.psem()
    sXbC = b.psem()
    sStat = b.psem()
    sRowC = b.psem()
    sBc = b.psem()
    sBcE = b.psem()
    sX1 = b.psem()
    sX1st = b.psem()

    b.waitr("sync", sA2Aatt, 1)
    for dht in range(NDT):
        stg = wo_stage[dht % 2]
        if dht >= 2:
            b.waitr("sync", sWoC, dht - 1)
        b.dma("sync", sWoL[dht % 2], stg, wo[dht * 128:(dht + 1) * 128, :])
        b.wait("vector", sWoL[dht % 2])
        sWoC.n += 1
        b.on("vector", lambda e, o=wo_bf[:, dht * D:(dht + 1) * D], i=stg, hh=sWoC.h:
             e.tensor_copy(out=o, in_=i).then_inc(hh, 1))
    for dht in range(NDT):
        b.dma("sync", sAtl, attl_sb[:, dht * TC:(dht + 1) * TC], ATTL[dht * 128:(dht + 1) * 128, :])

    b.wait("tensor", sWoC)
    b.wait("tensor", sAtl)

    pO = [ps[0], ps[1]]
    pSx, pSxx = ps[4], ps[5]
    pB1, pB2 = ps[6], ps[7]

    for tch in range(2):
        if tch >= 1:
            b.waitr("tensor", sRowC, 3 * tch)
        for dm in range(NDT):
            gi = tch * NDT + dm
            bank = pO[gi % 2]
            if gi >= 2:
                b.waitr("tensor", sX1p, gi - 1)
            for dht in range(NDT):
                mm(b, bank[:, :], wo_bf[:, dht * D + dm * 128:dht * D + (dm + 1) * 128],
                   attl_sb[:, dht * TC + tch * KC:dht * TC + tch * KC + KC],
                   dht == 0, dht == NDT - 1, inc=(sPO.h, 1) if dht == NDT - 1 else None)
            sPO.n += 1
            xrb = xres_buf[gi % 2]
            if gi >= 2:
                b.waitr("sync", sX1p, gi - 1)
            b.dma("sync", sXr[gi % 2], xrb, xtf[dm * 128:(dm + 1) * 128, tch * KC:(tch + 1) * KC])
            b.wait("vector", sPO)
            b.wait("vector", sXr[gi % 2])
            sX1p.n += 1
            b.on("vector", lambda e, o=x1pre[:, dm * KC:(dm + 1) * KC], i0=bank[:, :], i1=xrb, hh=sX1p.h:
                 e.tensor_tensor(out=o, in0=i0, in1=i1, op=ALU.add).then_inc(hh, 1))
            sqb = sq_buf[gi % 2]
            xbb = xb_buf[gi % 2]
            if gi >= 2:
                b.waitr("scalar", sStat, 2 * (gi - 1))
                b.waitr("vector", sStat, 2 * (gi - 1))
            b.wait("scalar", sX1p)
            sSqE.n += 1
            b.on("scalar", lambda e, o=sqb, i=x1pre[:, dm * KC:(dm + 1) * KC], hh=sSqE.h:
                 e.activation(out=o, in_=i, func=AF.Square).then_inc(hh, 1))
            b.wait("vector", sX1p)
            sXbC.n += 1
            b.on("vector", lambda e, o=xbb, i=x1pre[:, dm * KC:(dm + 1) * KC], hh=sXbC.h:
                 e.tensor_copy(out=o, in_=i).then_inc(hh, 1))
            b.wait("tensor", sXbC)
            mm(b, pSx[0:1, :], sb_ones, xbb, dm == 0, dm == NDT - 1,
               inc=(sStat.h, 1))
            b.wait("tensor", sSqE)
            mm(b, pSxx[0:1, :], sb_ones, sqb, dm == 0, dm == NDT - 1,
               inc=(sStat.h, 1))
            sStat.n += 2
        b.wait("vector", sStat)
        b.on("vector", lambda e: e.tensor_scalar_mul(out=R2(0), in0=pSx[0:1, :], scalar1=1.0 / D))
        b.on("vector", lambda e: e.tensor_scalar_mul(out=R2(1), in0=pSxx[0:1, :], scalar1=1.0 / D))
        b.on("vector", lambda e: e.tensor_tensor(out=R2(2), in0=R2(0), in1=R2(0), op=ALU.mult))
        sRowC.n += 1
        b.on("vector", lambda e, hh=sRowC.h: e.tensor_sub(out=R2(3), in0=R2(1), in1=R2(2)).then_inc(hh, 1))
        b.wait("scalar", sRowC)
        b.on("scalar", lambda e: e.activation(out=R2(4), in_=R2(3), func=AF.Ln, bias=cEPSLN))
        sRowC.n += 1
        b.on("scalar", lambda e, hh=sRowC.h: e.activation(out=R2(5), in_=R2(4), func=AF.Exp, scale=cNHALF).then_inc(hh, 1))
        b.wait("vector", sRowC)
        sRowC.n += 1
        b.on("vector", lambda e, hh=sRowC.h: e.tensor_tensor(out=R2(6), in0=R2(0), in1=R2(5), op=ALU.mult).then_inc(hh, 1))
        b.wait("sync", sRowC)
        b.dma("sync", sBc, bc1c[0:1, :], rows2[0:1, 5 * KC:7 * KC])
        b.wait("sync", sBc)
        b.dma("sync", sBcE, rstd_b, bc1c[0:1, 0:KC].broadcast_to([128, KC]))
        b.dma("sync", sBcE, mrow_b, bc1c[0:1, KC:2 * KC].broadcast_to([128, KC]))
        b.wait("vector", sBcE)
        for dm in range(NDT):
            src = x1pre[:, dm * KC:(dm + 1) * KC]
            t0f = xres_buf[dm % 2]
            if dm >= 2:
                b.wait("vector", sStp[dm % 2])
            sX1.n += 1
            b.on("vector", lambda e, o=t0f, i=src: e.tensor_tensor(out=o, in0=i, in1=rstd_b, op=ALU.mult))
            b.on("vector", lambda e, o=t0f, i=t0f, hh=sX1.h: e.tensor_sub(out=o, in0=i, in1=mrow_b).then_inc(hh, 1))
            b.wait("sync", sX1)
            b.dma("sync", sStp[dm % 2], X1F[dm * 128:(dm + 1) * 128, tch * KC:(tch + 1) * KC], t0f)
            xbc = sq_buf[dm % 2]
            b.wait("vector", sX1)
            sX1.n += 1
            b.on("vector", lambda e, o=xbc, i=t0f, hh=sX1.h: e.tensor_copy(out=o, in_=i).then_inc(hh, 1))
            b.wait("sync", sX1)
            b.dma("sync", sStp[dm % 2], X1B[dm * 128:(dm + 1) * 128, tch * KC:(tch + 1) * KC], xbc)
        b.wait("sync", sStp[0])
        b.wait("sync", sStp[1])

    b.tap("x1f", X1F, D, TC, F32, gate_sem=sX1st)
    b.tap("x1b", X1B, D, TC, BF16, gate_sem=sX1st)
    C["X1F"], C["X1B"] = X1F, X1B
    C["sStp"] = sStp
    arena.release()


def emit_lut16(b, idxb, acc, tmp, lut_col, sDq):
    b.on("vector", lambda e, o=acc, i=idxb, s2=lut_col(0): e.tensor_scalar(
        out=o, in0=i, scalar1=0.0, scalar2=s2, op0=ALU.is_equal, op1=ALU.mult))
    for v in range(1, 16):
        b.on("vector", lambda e, o=tmp, i=idxb, vv=float(v), s2=lut_col(v): e.tensor_scalar(
            out=o, in0=i, scalar1=vv, scalar2=s2, op0=ALU.is_equal, op1=ALU.mult))
        if v < 15:
            b.on("vector", lambda e, o=acc, i=tmp: e.tensor_add(out=o, in0=o, in1=i))
        else:
            sDq.n += 1
            b.on("vector", lambda e, o=acc, i=tmp, hh=sDq.h: e.tensor_add(out=o, in0=o, in1=i).then_inc(hh, 1))


def build_d0(b, C):
    b.pool_reset()
    """L0 ghost-FFN dequant (this core's shard) + AllGathers."""
    arena, ps = C["arena"], C["ps"]
    sCst, sGP = C["sCst"], C["sGP"]

    g_idxT = b.din("g_idxT", [HPC * DH, I0], I32)
    u_idxT = b.din("u_idxT", [HPC * DH, I0], I32)
    d_idxT = b.din("d_idxT", [688, D], I32)
    lutb = b.din("lutb", [128, 16], F32)

    NT0 = I0 // 128
    ag_w0gu_in = b.dint("ag_w0gu_in", [2 * 2 * NT0 * 128, 128], BF16)
    W0GUT = b.dint("W0GUT", [W * 2 * 2 * NT0 * 128, 128], BF16)
    ag_w0d_in = b.dint("ag_w0d_in", [688, D], BF16)
    W0DT = b.dint("W0DT", [I0, D], BF16)
    sAGw0 = b.sem("sAGw0")

    arena.mark()
    sb_lut = arena.alloc(16, F32)
    idx_st = [arena.alloc(1024, I32) for _ in range(2)]
    idxb = arena.alloc(1024, BF16)
    tmp = arena.alloc(1024, BF16)
    acc = [arena.alloc(1024, BF16) for _ in range(2)]
    sDqL = b.psem2()
    sDq = b.psem()
    sDqS = [b.sem("sDqS0"), b.sem("sDqS1")]
    sLut = b.psem()

    for p in (0, 1):
        b.wait("sync", C["sStp"][p])
    b.dma("sync", sLut, sb_lut, lutb[:, :])

    def lut_col(v):
        return sb_lut[:, v:v + 1]

    nchunk = 0

    def dq_chunk(src_hbm, r0, pr, c0, cols, store_fn):
        nonlocal nchunk
        p = nchunk % 2
        ist = idx_st[p]
        if nchunk >= 2:
            b.waitr("sync", sDq, nchunk - 1)
        b.dma("sync", sDqL[p], ist[0:pr, 0:cols], src_hbm[r0:r0 + pr, c0:c0 + cols])
        b.wait("vector", sDqL[p])
        if nchunk == 0:
            b.wait("vector", sLut)
        a = acc[p]
        if nchunk >= 2:
            b.wait("vector", sDqS[p])
        b.on("vector", lambda e, o=idxb[0:pr, 0:cols], i=ist[0:pr, 0:cols]: e.tensor_copy(out=o, in_=i))
        emit_lut16(b, idxb[0:pr, 0:cols], a[0:pr, 0:cols], tmp[0:pr, 0:cols],
                   lambda v, pr=pr: sb_lut[0:pr, v:v + 1], sDq)
        b.wait("gpsimd", sDq)
        store_fn(a, sDqS[p])
        nchunk += 1

    for src, base_it in ((g_idxT, 0), (u_idxT, NT0)):
        for mtl in range(2):
            it0 = 0
            while it0 < NT0:
                csz = min(8, NT0 - it0)
                def store(a, ssem, mtl=mtl, base_it=base_it, it0=it0, csz=csz):
                    for t in range(csz):
                        row = (mtl * 2 * NT0 + base_it + it0 + t) * 128
                        b.dma("gpsimd", ssem,
                              ag_w0gu_in[row:row + 128, :], a[:, t * 128:(t + 1) * 128])
                dq_chunk(src, mtl * 128, 128, it0 * 128, csz * 128, store)
                it0 += csz
    for rg in range(6):
        r0 = rg * 128
        pr = min(128, 688 - r0)
        for half in range(2):
            def store(a, ssem, r0=r0, pr=pr, half=half):
                b.dma("gpsimd", ssem, ag_w0d_in[r0:r0 + pr, half * 1024:(half + 1) * 1024],
                      a[0:pr, 0:1024])
            dq_chunk(d_idxT, r0, pr, half * 1024, 1024, store)

    b.wait("gpsimd", sDqS[0])
    b.wait("gpsimd", sDqS[1])
    b.collective("AllGather", ALU.bypass, ag_w0gu_in[:, :], W0GUT[:, :], sAGw0)
    b.collective("AllGather", ALU.bypass, ag_w0d_in[:, :], W0DT[:, :], sAGw0)
    arena.release()
    C.update(W0GUT=W0GUT, W0DT=W0DT, sAGw0=sAGw0)
    b.tap("w0gut", W0GUT, 4096, 128, BF16)
    b.tap("w0dt", W0DT, I0, D, BF16)


class LnCtx:

    def __init__(self, b, C, name):
        arena, ps = C["arena"], C["ps"]
        self.b = b
        self.C = C
        self.xpre = arena.alloc(NDT * KC, F32)
        self.res_buf = [arena.alloc(KC, F32) for _ in range(2)]
        self.sq_buf = [arena.alloc(KC, BF16) for _ in range(2)]
        self.xb_buf = [arena.alloc(KC, BF16) for _ in range(2)]
        self.rows = arena.alloc(4 * KC, F32)
        self.rstd_b = arena.alloc(KC, F32)
        self.mrow_b = arena.alloc(KC, F32)
        self.sAdd = b.psem()
        self.sSqE = b.psem()
        self.sXbC = b.psem()
        self.sStat = b.psem()
        self.sRowC = b.psem()
        self.sBc = b.psem()
        self.sBcE = b.psem()
        self.sX = b.psem()
        self.sSt = b.psem2()
        self.sLr = b.psem2()
        self.bc_dram = b.dint(name + "_bcd", [1, 2 * KC], F32)

    def R(self, slot):
        return self.rows[0:1, slot * KC:(slot + 1) * KC]

    def add_tile(self, dm, pre_ap, res_hbm, rescol):
        b = self.b
        xrb = self.res_buf[dm % 2]
        if self.sAdd.n >= 2:
            b.wait("sync", self.sAdd, self.sAdd.n - 1)
        b.dma("sync", self.sLr[dm % 2], xrb, res_hbm[dm * 128:(dm + 1) * 128, rescol:rescol + KC])
        b.wait("vector", self.sLr[dm % 2])
        self.sAdd.n += 1
        b.on("vector", lambda e, o=self.xpre[:, dm * KC:(dm + 1) * KC], i0=pre_ap, i1=xrb, hh=self.sAdd.h:
             e.tensor_tensor(out=o, in0=i0, in1=i1, op=ALU.add).then_inc(hh, 1))

    def stats_norm_store(self, outs, rms=False, dbg=None):
        b = self.b
        C = self.C
        if dbg is not None:
            hbm, col0 = dbg
            b.wait("sync", self.sAdd)
            for dm in range(NDT):
                b.dma("sync", self.sSt[0], hbm[dm * 128:(dm + 1) * 128, col0:col0 + KC],
                      self.xpre[:, dm * KC:(dm + 1) * KC])
            b.wait("sync", self.sSt[0])
        ps = C["ps"]
        sb_ones, sb_ones1f = C["sb_ones"], C["sb_ones1f"]
        cEPS = C["sb_consts"][0:1, 0:1] if not rms else C["sb_consts"][0:1, 1:2]
        cNHALF = C["sb_consts"][0:1, 2:3]
        pSx, pSxx = ps[4], ps[5]
        pB1, pB2 = ps[6], ps[7]
        R = self.R
        b.wait("scalar", self.sAdd)
        b.wait("vector", self.sAdd)
        for dm in range(NDT):
            sqb = self.sq_buf[dm % 2]
            xbb = self.xb_buf[dm % 2]
            if dm >= 2:
                b.wait("scalar", self.sStat, self.sStat.n - (2 if not rms else 1))
                if not rms:
                    b.wait("vector", self.sStat, self.sStat.n - 2)
            src = self.xpre[:, dm * KC:(dm + 1) * KC]
            self.sSqE.n += 1
            b.on("scalar", lambda e, o=sqb, i=src, hh=self.sSqE.h:
                 e.activation(out=o, in_=i, func=AF.Square).then_inc(hh, 1))
            if not rms:
                self.sXbC.n += 1
                b.on("vector", lambda e, o=xbb, i=src, hh=self.sXbC.h:
                     e.tensor_copy(out=o, in_=i).then_inc(hh, 1))
                b.wait("tensor", self.sXbC)
                mm(b, pSx[0:1, :], sb_ones, xbb, dm == 0, dm == NDT - 1, inc=(self.sStat.h, 1))
                self.sStat.n += 1
            b.wait("tensor", self.sSqE)
            mm(b, pSxx[0:1, :], sb_ones, sqb, dm == 0, dm == NDT - 1, inc=(self.sStat.h, 1))
            self.sStat.n += 1
        b.wait("vector", self.sStat)
        if not rms:
            b.on("vector", lambda e: e.tensor_scalar_mul(out=R(0), in0=pSx[0:1, :], scalar1=1.0 / D))
        b.on("vector", lambda e: e.tensor_scalar_mul(out=R(1), in0=pSxx[0:1, :], scalar1=1.0 / D))
        if not rms:
            b.on("vector", lambda e: e.tensor_tensor(out=R(2), in0=R(0), in1=R(0), op=ALU.mult))
            self.sRowC.n += 1
            b.on("vector", lambda e, hh=self.sRowC.h: e.tensor_sub(out=R(3), in0=R(1), in1=R(2)).then_inc(hh, 1))
        else:
            self.sRowC.n += 1
            b.on("vector", lambda e, hh=self.sRowC.h: e.tensor_copy(out=R(3), in_=R(1)).then_inc(hh, 1))
        b.wait("scalar", self.sRowC)
        b.on("scalar", lambda e: e.activation(out=R(1), in_=R(3), func=AF.Ln, bias=cEPS))
        self.sRowC.n += 1
        b.on("scalar", lambda e, hh=self.sRowC.h: e.activation(out=R(2), in_=R(1), func=AF.Exp, scale=cNHALF).then_inc(hh, 1))
        b.wait("vector", self.sRowC)
        if not rms:
            self.sRowC.n += 1
            b.on("vector", lambda e, hh=self.sRowC.h: e.tensor_tensor(out=R(3), in0=R(0), in1=R(2), op=ALU.mult).then_inc(hh, 1))
        b.wait("sync", self.sRowC)
        b.dma("sync", self.sBc, self.bc_dram[0:1, 0:(KC if rms else 2 * KC)],
              self.rows[0:1, 2 * KC:(3 * KC if rms else 4 * KC)])
        b.wait("sync", self.sBc)
        b.dma("sync", self.sBcE, self.rstd_b, self.bc_dram[0:1, 0:KC].broadcast_to([128, KC]))
        if not rms:
            b.dma("sync", self.sBcE, self.mrow_b, self.bc_dram[0:1, KC:2 * KC].broadcast_to([128, KC]))
        b.wait("vector", self.sBcE)
        for dm in range(NDT):
            src = self.xpre[:, dm * KC:(dm + 1) * KC]
            t0f = self.res_buf[dm % 2]
            if dm >= 2:
                b.wait("vector", self.sSt[dm % 2])
            self.sX.n += 1
            if not rms:
                b.on("vector", lambda e, o=t0f, i=src: e.tensor_tensor(out=o, in0=i, in1=self.rstd_b, op=ALU.mult))
                b.on("vector", lambda e, o=t0f, hh=self.sX.h: e.tensor_sub(out=o, in0=o, in1=self.mrow_b).then_inc(hh, 1))
            else:
                b.on("vector", lambda e, o=t0f, i=src, hh=self.sX.h: e.tensor_tensor(out=o, in0=i, in1=self.rstd_b, op=ALU.mult).then_inc(hh, 1))
            need_bf = any(dt == BF16 for _, _, dt in outs)
            xbc = self.sq_buf[dm % 2]
            if need_bf:
                b.wait("vector", self.sX)
                self.sX.n += 1
                b.on("vector", lambda e, o=xbc, i=t0f, hh=self.sX.h: e.tensor_copy(out=o, in_=i).then_inc(hh, 1))
            b.wait("sync", self.sX)
            for hbm, col0, dt in outs:
                srcv = t0f if dt == F32 else xbc
                b.dma("sync", self.sSt[dm % 2], hbm[dm * 128:(dm + 1) * 128, col0:col0 + KC], srcv)
        b.wait("sync", self.sSt[0])
        b.wait("sync", self.sSt[1])

    def rms_scale_inplace(self):
        b = self.b
        C = self.C
        ps = C["ps"]
        sb_ones = C["sb_ones"]
        cEPS = C["sb_consts"][0:1, 1:2]
        cNHALF = C["sb_consts"][0:1, 2:3]
        pSxx = ps[5]
        R = self.R
        b.wait("scalar", self.sAdd)
        for dm in range(NDT):
            sqb = self.sq_buf[dm % 2]
            if dm >= 2:
                b.wait("scalar", self.sStat, self.sStat.n - 1)
            src = self.xpre[:, dm * KC:(dm + 1) * KC]
            self.sSqE.n += 1
            b.on("scalar", lambda e, o=sqb, i=src, hh=self.sSqE.h:
                 e.activation(out=o, in_=i, func=AF.Square).then_inc(hh, 1))
            b.wait("tensor", self.sSqE)
            mm(b, pSxx[0:1, :], sb_ones, sqb, dm == 0, dm == NDT - 1, inc=(self.sStat.h, 1))
            self.sStat.n += 1
        b.wait("vector", self.sStat)
        b.on("vector", lambda e: e.tensor_scalar_mul(out=R(1), in0=pSxx[0:1, :], scalar1=1.0 / D))
        self.sRowC.n += 1
        b.on("vector", lambda e, hh=self.sRowC.h: e.tensor_copy(out=R(3), in_=R(1)).then_inc(hh, 1))
        b.wait("scalar", self.sRowC)
        b.on("scalar", lambda e: e.activation(out=R(1), in_=R(3), func=AF.Ln, bias=cEPS))
        self.sRowC.n += 1
        b.on("scalar", lambda e, hh=self.sRowC.h:
             e.activation(out=R(2), in_=R(1), func=AF.Exp, scale=cNHALF).then_inc(hh, 1))
        b.wait("sync", self.sRowC)
        b.dma("sync", self.sBc, self.bc_dram[0:1, 0:KC], self.rows[0:1, 2 * KC:3 * KC])
        b.wait("sync", self.sBc)
        b.dma("sync", self.sBcE, self.rstd_b, self.bc_dram[0:1, 0:KC].broadcast_to([128, KC]))
        b.wait("vector", self.sBcE)
        for dm in range(NDT):
            src = self.xpre[:, dm * KC:(dm + 1) * KC]
            b.on("vector", lambda e, o=src, i=src:
                 e.tensor_tensor(out=o, in0=i, in1=self.rstd_b, op=ALU.mult))


def build_p2(b, C):
    b.pool_reset()
    """L0 ghost FFN: gate/up (silu*), down, +residual, LN."""
    arena, ps = C["arena"], C["ps"]
    sGP = C["sGP"]
    W0GUT, W0DT, sAGw0 = C["W0GUT"], C["W0DT"], C["sAGw0"]
    X1F, X1B = C["X1F"], C["X1B"]
    NT0 = I0 // 128

    X2F = b.dint("X2F", [D, TC], F32)
    X2PRE = b.dint("X2PRE", [D, TC], F32)
    ag_xg_in = b.dint("ag_xg_in", [D, TC], BF16)
    XG = b.dint("XG", [W * D, TC], BF16)
    sAGxg = b.sem("sAGxg")

    W0DTt = b.dint("W0DTt", [NT0 * 16 * 128, 128], BF16)
    sRpk = b.sem("sRpk")

    arena.mark()
    xb1_sb = arena.alloc(NDT * TC, BF16)
    H1 = arena.alloc(NT0 * TC, BF16)
    ln = LnCtx(b, C, "ln2")
    arena.mark()
    wbuf = [arena.alloc(32 * 128, BF16) for _ in range(2)]
    sg_buf = [arena.alloc(KC, BF16) for _ in range(2)]

    sP2g = b.psem()
    sSil = b.psem()
    sHE = b.psem()
    sDnG = b.psem()
    sX1bL = b.psem()
    sW0L = b.psem2()
    sWcL = b.psem2()

    b.wait("gpsimd", sAGw0, 2)
    for it in range(NT0):
        for hf in range(2):
            r0 = it * 16 * 128 + hf * 8 * 128
            b.dma("gpsimd", sRpk,
                  W0DTt[r0:r0 + 8 * 128, :].rearrange("(a p) c -> a p c", p=128),
                  W0DT[it * 128:(it + 1) * 128, hf * 1024:(hf + 1) * 1024].rearrange("p (a c) -> a p c", a=8))

    b.wait("sync", sAGw0, 2)
    for mt in range(NDT):
        b.dma("sync", sX1bL, xb1_sb[:, mt * TC:(mt + 1) * TC], X1B[mt * 128:(mt + 1) * 128, :])
    b.wait("tensor", sX1bL)

    for it in range(NT0):
        wb = wbuf[it % 2]
        if it >= 2:
            b.waitr("sync", sP2g, 4 * (it - 1))
        for mt in range(NDT):
            row = (mt * 2 * NT0 + it) * 128
            b.dma("sync", sW0L[it % 2], wb[:, mt * 128:(mt + 1) * 128], W0GUT[row:row + 128, :])
        for mt in range(NDT):
            row = (mt * 2 * NT0 + NT0 + it) * 128
            b.dma("sync", sW0L[it % 2], wb[:, (16 + mt) * 128:(17 + mt) * 128], W0GUT[row:row + 128, :])
        b.wait("tensor", sW0L[it % 2])
        for tch in range(2):
            for gu in range(2):
                bank = ps[(it % 2) * 4 + tch * 2 + gu]
                if it >= 2:
                    if gu == 0:
                        b.waitr("tensor", sSil, 2 * (it - 1))
                    else:
                        b.waitr("tensor", sHE, 2 * (it - 1))
                for mt in range(NDT):
                    mm(b, bank[:, :], wb[:, (gu * 16 + mt) * 128:(gu * 16 + mt + 1) * 128],
                       xb1_sb[:, mt * TC + tch * KC:mt * TC + tch * KC + KC],
                       mt == 0, mt == NDT - 1, inc=(sP2g.h, 1) if mt == NDT - 1 else None)
                sP2g.n += 1
        for tch in range(2):
            gbank = ps[(it % 2) * 4 + tch * 2 + 0]
            ubank = ps[(it % 2) * 4 + tch * 2 + 1]
            sgb = sg_buf[tch]
            k = 2 * it + tch
            if k >= 2:
                b.waitr("scalar", sHE, k - 1)
            b.waitr("scalar", sP2g, 4 * it + tch * 2 + 1)
            sSil.n += 1
            b.on("scalar", lambda e, o=sgb, i=gbank[:, :], hh=sSil.h:
                 e.activation(out=o, in_=i, func=AF.Silu).then_inc(hh, 1))
            b.wait("vector", sSil)
            b.waitr("vector", sP2g, 4 * it + tch * 2 + 2)
            sHE.n += 1
            b.on("vector", lambda e, o=H1[:, it * TC + tch * KC:it * TC + tch * KC + KC],
                 i0=sgb, i1=ubank[:, :], hh=sHE.h:
                 e.tensor_tensor(out=o, in0=i0, in1=i1, op=ALU.mult).then_inc(hh, 1))

    if os.environ.get("P2_GATEUP_ONLY"):
        b.wait("sync", sHE)
        arena.release()
        C.update(X2F=X2F, XG=XG, sAGxg=sAGxg)
        return
    arena.release()
    wcol = [arena.alloc(NT0 * 128, BF16) for _ in range(2)]
    b.wait("tensor", sHE)
    b.wait("sync", sRpk)
    b.wait("sync", sP2g)
    b.wait("sync", sHE)
    b.wait("scalar", sRpk)
    b.wait("scalar", sP2g)
    b.wait("scalar", sHE)
    for tch in range(2):
        if tch >= 1:
            b.wait("tensor", ln.sBcE)
        for g in range(2):
            if tch + g > 0:
                b.wait("tensor", ln.sAdd)
            for dml in range(8):
                dm = g * 8 + dml
                gidx = (tch * 2 + g) * 8 + dml
                wc = wcol[gidx % 2]
                q = "sync" if gidx % 2 == 0 else "scalar"
                qs = sWcL[gidx % 2]
                if gidx >= 2:
                    b.waitr(q, sDnG, gidx - 1)
                for i in range(NT0):
                    b.dma(q, qs, wc[:, i * 128:(i + 1) * 128],
                          W0DTt[(i * 16 + dm) * 128:(i * 16 + dm + 1) * 128, :])
                b.wait("tensor", qs)
                for i in range(NT0):
                    mm(b, ps[dml][:, :], wc[:, i * 128:(i + 1) * 128],
                       H1[:, i * TC + tch * KC:i * TC + tch * KC + KC],
                       i == 0, i == NT0 - 1,
                       inc=(sDnG.h, 1) if i == NT0 - 1 else None)
                sDnG.n += 1
            b.wait("vector", sDnG)
            for dml in range(8):
                dm = g * 8 + dml
                ln.add_tile(dm, ps[dml][:, :], X1F, tch * KC)
        ln.stats_norm_store([(X2F, tch * KC, F32), (ag_xg_in, tch * KC, BF16)],
                            dbg=(X2PRE, tch * KC) if "x2pre" in b.taps else None)
    b.wait("gpsimd", ln.sSt[0])
    b.wait("gpsimd", ln.sSt[1])
    b.collective("AllGather", ALU.bypass, ag_xg_in[:, :], XG[:, :], sAGxg)
    arena.release()
    C.update(X2F=X2F, XG=XG, sAGxg=sAGxg)
    b.tap("x2f", X2F, D, TC, F32, gate_sem=ln.sSt[0])
    b.tap("x2pre", X2PRE, D, TC, F32, gate_sem=ln.sSt[0])
    b.tap("xgin", ag_xg_in, D, TC, BF16, gate_sem=ln.sSt[0])


def build_p3(b, C):
    b.pool_reset()
    """L1 Mamba: xz proj, conv, dbc AllReduce, selective scan, out-proj, LN."""
    arena, ps = C["arena"], C["ps"]
    sGP = C["sGP"]
    XG, sAGxg = C["XG"], C["sAGxg"]
    X2F = C["X2F"]

    winxT = b.din("winxT", [D, EC], F32)
    winzT = b.din("winzT", [D, EC], F32)
    convw4 = b.din("convw4", [128, 16], F32)
    convb4 = b.din("convb4", [128, 4], F32)
    wxT = b.din("wxT", [EC, 160], F32)
    wdtT = b.din("wdtT", [RDT, EC], F32)
    bdt4 = b.din("bdt4", [128, 4], F32)
    alog4 = b.din("alog4", [128, 4 * NS], F32)
    dp4 = b.din("dp4", [128, 4], F32)
    woutT_t = b.din("woutT_t", [EC * 16, 128], F32)

    DBCP = b.dint("DBCP", [160, T], F32)
    DBC = b.dint("DBC", [160, T], F32)
    BCB = b.dint("BCB", [2 * NS, T], BF16)
    ag_yg_in = b.dint("ag_yg_in", [E4, TC], BF16)
    YGL = b.dint("YGL", [E4, TC], BF16)
    ag_wout_in = b.dint("ag_wout_in", [EC * 16, 128], BF16)
    WOUTT = b.dint("WOUTT", [E4 * 16, 128], BF16)
    X3F = b.dint("X3F", [D, TC], F32)
    X3B = b.dint("X3B", [D, TC], BF16)
    sARdbc = b.sem("sARdbc")
    sA2Ayg = b.sem("sA2Ayg")
    sAGwo2 = b.sem("sAGwo2")

    arena.mark()
    winz_bf = arena.alloc(NDT * KC, BF16)
    wdt_bf = arena.alloc(EC, BF16)
    A4 = arena.alloc(4 * NS, F32)
    sb_convw = arena.alloc(16, F32)
    sb_convb = arena.alloc(4, F32)
    sb_bdt = arena.alloc(4, F32)
    sb_alog = arena.alloc(4 * NS, F32)
    sb_dp = arena.alloc(4, F32)
    XC = arena.alloc(4 * T, BF16)
    xab = [arena.alloc(NDT * KC, BF16) for _ in range(2)]

    sWp = b.psem()
    sWoS = b.psem2()
    sWpL = b.psem2()
    sVec = b.psem()

    arena.mark()
    stg = [arena.alloc(KC, F32) for _ in range(2)]
    winx_bf = arena.alloc(NDT * KC, BF16)
    wx_bf = arena.alloc(4 * 160, BF16)
    wob_st = [arena.alloc(128, BF16) for _ in range(2)]
    xp_t = [arena.alloc(520, BF16) for _ in range(4)]
    accv2 = [arena.alloc(KC, F32) for _ in range(2)]
    dbcb = [arena.alloc(KC, F32) for _ in range(2)]
    dbcb2 = [arena.alloc(KC, F32) for _ in range(2)]

    b.wait("sync", sAGxg, 1)
    def cast_load(src_ap, dst_view, cols):
        if sWp.n >= 2:
            b.wait("sync", sWp, sWp.n - 1)
        p = sWp.n % 2
        sg = stg[p]
        b.dma("sync", sWpL[p], sg[:, 0:cols], src_ap)
        b.wait("vector", sWpL[p])
        sWp.n += 1
        b.on("vector", lambda e, o=dst_view, i=sg[:, 0:cols], hh=sWp.h:
             e.tensor_copy(out=o, in_=i).then_inc(hh, 1))

    for mt in range(NDT):
        cast_load(winxT[mt * 128:(mt + 1) * 128, :], winx_bf[:, mt * KC:(mt + 1) * KC], KC)
    for mt in range(NDT):
        cast_load(winzT[mt * 128:(mt + 1) * 128, :], winz_bf[:, mt * KC:(mt + 1) * KC], KC)
    for et in range(4):
        cast_load(wxT[et * 128:(et + 1) * 128, :], wx_bf[:, et * 160:(et + 1) * 160], 160)
    cast_load(wdtT[:, :], wdt_bf, EC)
    b.dma("sync", sVec, sb_convw, convw4[:, :])
    b.dma("sync", sVec, sb_convb, convb4[:, :])
    b.dma("sync", sVec, sb_bdt, bdt4[:, :])
    b.dma("sync", sVec, sb_alog, alog4[:, :])
    b.dma("sync", sVec, sb_dp, dp4[:, :])
    b.wait("scalar", sVec)
    sWp.n += 1
    b.on("scalar", lambda e, hh=sWp.h: e.activation(out=A4, in_=sb_alog, func=AF.Exp).then_inc(hh, 1))
    b.wait("vector", sWp)
    sWp.n += 1
    b.on("vector", lambda e, hh=sWp.h: e.tensor_scalar_mul(out=A4, in0=A4, scalar1=-1.0).then_inc(hh, 1))
    for i in range(EC * 16 // 128):
        p = i % 2
        wst = wob_st[p]
        if i >= 2:
            b.wait("sync", sWp, sWp.n - 1)
            b.wait("vector", sWoS[p])
        sg = stg[p]
        b.dma("sync", sWpL[p], sg[:, 0:128], woutT_t[i * 128:(i + 1) * 128, :])
        b.wait("vector", sWpL[p])
        sWp.n += 1
        b.on("vector", lambda e, o=wst, i2=sg[:, 0:128], hh=sWp.h:
             e.tensor_copy(out=o, in_=i2).then_inc(hh, 1))
        b.wait("sync", sWp)
        b.dma("sync", sWoS[p], ag_wout_in[i * 128:(i + 1) * 128, :], wst)

    sXP = b.psem()
    sXPe = b.psem()
    sCv = b.psem()
    sXC = b.psem()
    sDB = b.psem()
    sDBe = b.psem()
    sDBs = [b.sem("sDBs0"), b.sem("sDBs1")]
    sXa = b.psem2()

    for et in range(4):
        b.on("vector", lambda e, o=xp_t[et][:, 0:3]: e.memset(o, 0.0))
    b.wait("tensor", sWp)

    for ch in range(NKC):
        xb = xab[ch % 2]
        if ch >= 2:
            b.waitr("sync", sXP, 4 * (ch - 1))
        r, c0 = ch // 2, (ch % 2) * KC
        for mt in range(NDT):
            b.dma("sync", sXa[ch % 2], xb[:, mt * KC:(mt + 1) * KC],
                  XG[r * D + mt * 128:r * D + (mt + 1) * 128, c0:c0 + KC])
        b.wait("tensor", sXa[ch % 2])
        for et in range(4):
            g = 4 * ch + et
            bank = ps[g % 2]
            if g >= 2:
                b.waitr("tensor", sXPe, g - 1)
            for mt in range(NDT):
                mm(b, bank[:, :], winx_bf[:, mt * KC + et * 128:mt * KC + (et + 1) * 128],
                   xb[:, mt * KC:(mt + 1) * KC], mt == 0, mt == NDT - 1,
                   inc=(sXP.h, 1) if mt == NDT - 1 else None)
            sXP.n += 1
            b.wait("vector", sXP)
            xpt = xp_t[et]
            if ch > 0:
                b.on("vector", lambda e, o=xpt[:, 0:3], i=xpt[:, 512:515]: e.tensor_copy(out=o, in_=i))
            sXPe.n += 1
            b.on("vector", lambda e, o=xpt[:, 3:515], i=bank[:, :], hh=sXPe.h:
                 e.tensor_copy(out=o, in_=i).then_inc(hh, 1))
            accv = accv2[g % 2]
            if g >= 2:
                b.waitr("vector", sXC, g - 1)
            b.on("vector", lambda e, o=accv, i=xpt[:, 0:512], s1=sb_convw[:, et * 4:et * 4 + 1],
                 s2=sb_convb[:, et:et + 1]:
                 e.tensor_scalar(out=o, in0=i, scalar1=s1, scalar2=s2, op0=ALU.mult, op1=ALU.add))
            for k in range(1, 4):
                inc = (sCv, 1) if k == 3 else None
                if inc:
                    sCv.n += 1
                def cop(e, o=accv, i=xpt[:, k:512 + k], s=sb_convw[:, et * 4 + k:et * 4 + k + 1], ii=inc):
                    ins = e.scalar_tensor_tensor(out=o, in0=i, scalar=s, in1=o, op0=ALU.mult, op1=ALU.add)
                    if ii:
                        ins.then_inc(ii[0].h, ii[1])
                b.on("vector", cop)
            b.wait("scalar", sCv)
            sXC.n += 1
            b.on("scalar", lambda e, o=XC[:, et * T + ch * KC:et * T + ch * KC + KC], i=accv, hh=sXC.h:
                 e.activation(out=o, in_=i, func=AF.Silu).then_inc(hh, 1))
        b.wait("tensor", sXC)
        for fp in range(2):
            bank = ps[4 + fp]
            pr = 128 if fp == 0 else 32
            if ch >= 2:
                b.waitr("tensor", sDBe, 2 * (ch - 1))
            for et in range(4):
                lh = wx_bf[:, et * 160:et * 160 + 128] if fp == 0 else wx_bf[:, et * 160 + 128:et * 160 + 160]
                mm(b, bank[0:pr, :], lh, XC[:, et * T + ch * KC:et * T + ch * KC + KC],
                   et == 0, et == 3, inc=(sDB.h, 1) if et == 3 else None)
            sDB.n += 1
            dbv = (dbcb if fp == 0 else dbcb2)[ch % 2]
            b.wait("vector", sDB)
            if ch >= 2:
                b.wait("vector", sDBs[ch % 2])
            sDBe.n += 1
            b.on("vector", lambda e, o=dbv[0:pr, :], i=bank[0:pr, :], hh=sDBe.h:
                 e.tensor_copy(out=o, in_=i).then_inc(hh, 1))
            b.wait("gpsimd", sDBe)
            r0 = 0 if fp == 0 else 128
            b.dma("gpsimd", sDBs[ch % 2], DBCP[r0:r0 + pr, ch * KC:(ch + 1) * KC], dbv[0:pr, :])

    b.wait("gpsimd", sDBs[0])
    b.wait("gpsimd", sDBs[1])
    b.collective("AllReduce", ALU.add, DBCP[:, :], DBC[:, :], sARdbc)
    b.wait("gpsimd", sWoS[0])
    b.wait("gpsimd", sWoS[1])
    b.collective("AllGather", ALU.bypass, ag_wout_in[:, :], WOUTT[:, :], sAGwo2)

    arena.release()

    sBCBc = b.psem()
    sBCBl = b.psem()
    sBCBs = b.psem()
    bcst = xab[0][:, :].bitcast(F32)
    bcstb = xab[1]
    b.waitr("sync", sARdbc, 1)
    for hv in range(2):
        b.dma("sync", sBCBl, bcst[0:32, 0:T // 2], DBC[128:160, hv * (T // 2):(hv + 1) * (T // 2)])
        b.wait("vector", sBCBl)
        sBCBc.n += 1
        b.on("vector", lambda e, hh=sBCBc.h, hv=hv: e.tensor_copy(
            out=bcstb[0:32, hv * (T // 2):(hv + 1) * (T // 2)], in_=bcst[0:32, 0:T // 2]).then_inc(hh, 1))
        b.wait("sync", sBCBc)
    b.dma("sync", sBCBs, BCB[:, :], bcstb[0:32, 0:T])
    b.wait("sync", sBCBs)

    arena.mark()
    dtc = [arena.alloc(KC, F32) for _ in range(2)]
    dtb = [arena.alloc(KC, BF16) for _ in range(2)]
    Bb = arena.alloc(NS * KC, BF16)
    Cb = arena.alloc(NS * KC, BF16)
    HS = arena.alloc(NS * KC, BF16)
    delta_r = arena.alloc(4 * KC, F32)
    u_r = arena.alloc(4 * KC, BF16)
    tA = arena.alloc(KC, F32)
    tB = arena.alloc(KC, F32)
    dAb = [arena.alloc(KC, F32) for _ in range(2)]
    dxb = [arena.alloc(KC, BF16) for _ in range(2)]
    st_carry = arena.alloc(4 * NS, F32)
    yb = arena.alloc(KC, F32)
    szb = [arena.alloc(KC, BF16) for _ in range(2)]
    ygb = [arena.alloc(KC, BF16) for _ in range(2)]

    sDT = b.psem()
    sDel = b.psem()
    sSP = b.psem()
    sDelC = b.psem()
    sDA = b.psem()
    sSCN = b.psem()
    sZ = b.psem()
    sSil3 = b.psem()
    sYG = b.psem()
    sYGs = [b.sem("sYGs0"), b.sem("sYGs1")]
    sPRD = b.psem()
    sDtL = b.psem2()
    sBCl = b.psem()

    b.on("vector", lambda e: e.memset(st_carry, 0.0))

    for ch in range(NKC):
        r, c0 = ch // 2, (ch % 2) * KC
        xb = xab[ch % 2]
        if ch >= 2:
            b.waitr("sync", sZ, 4 * (ch - 1))
        for mt in range(NDT):
            b.dma("sync", sXa[ch % 2], xb[:, mt * KC:(mt + 1) * KC],
                  XG[r * D + mt * 128:r * D + (mt + 1) * 128, c0:c0 + KC])
        if ch >= 2:
            b.waitr("sync", sDel, 4 * (ch - 1))
        b.dma("sync", sDtL[ch % 2], dtc[ch % 2], DBC[0:128, ch * KC:(ch + 1) * KC])
        if ch >= 1:
            b.waitr("sync", sPRD, 4 * ch)
        for s in range(NS):
            b.dma("sync", sBCl, Bb[:, s * KC:(s + 1) * KC],
                  BCB[s:s + 1, ch * KC:(ch + 1) * KC].broadcast_to([128, KC]))
            b.dma("sync", sBCl, Cb[:, s * KC:(s + 1) * KC],
                  BCB[NS + s:NS + s + 1, ch * KC:(ch + 1) * KC].broadcast_to([128, KC]))
        b.wait("vector", sXa[ch % 2])
        b.wait("vector", sDtL[ch % 2])
        b.wait("vector", sBCl)
        if ch >= 2:
            b.waitr("vector", sDel, 4 * (ch - 1))
        sDT.n += 1
        b.on("vector", lambda e, o=dtb[ch % 2], i=dtc[ch % 2], hh=sDT.h:
             e.tensor_copy(out=o, in_=i).then_inc(hh, 1))
        b.wait("tensor", sDT)
        b.wait("tensor", sXa[ch % 2])
        for et in range(4):
            bank = ps[6 + et % 2]
            if 4 * ch + et >= 2:
                b.waitr("tensor", sSP, 2 * (4 * ch + et - 1))
            sDel.n += 1
            mm(b, bank[:, :], wdt_bf[:, et * 128:(et + 1) * 128], dtb[ch % 2],
               True, True, inc=(sDel.h, 1))
            b.wait("scalar", sDel)
            if 4 * ch + et >= 1:
                b.waitr("scalar", sDelC, 4 * ch + et)
            bcol = sb_bdt[:, et:et + 1]
            b.on("scalar", lambda e, o=tA, i=bank[:, :], bc=bcol:
                 e.activation(out=o, in_=i, func=AF.Abs, bias=bc))
            b.on("scalar", lambda e, o=tB, i=tA: e.activation(out=o, in_=i, func=AF.Exp, scale=-1.0))
            sSP.n += 1
            b.on("scalar", lambda e, o=tA, i=tB, hh=sSP.h:
                 e.activation(out=o, in_=i, func=AF.Ln, bias=1.0).then_inc(hh, 1))
            sSP.n += 1
            b.on("scalar", lambda e, o=tB, i=bank[:, :], bc=bcol, hh=sSP.h:
                 e.activation(out=o, in_=i, func=AF.Relu, bias=bc).then_inc(hh, 1))
            dl = delta_r[:, et * KC:(et + 1) * KC]
            b.wait("vector", sSP)
            sDelC.n += 1
            b.on("vector", lambda e, o=dl, i0=tA, i1=tB, hh=sDelC.h:
                 e.tensor_add(out=o, in0=i0, in1=i1).then_inc(hh, 1))
            b.wait("vector", sDelC)
            b.on("vector", lambda e, o=u_r[:, et * KC:(et + 1) * KC], i0=dl,
                 i1=XC[:, et * T + ch * KC:et * T + ch * KC + KC]:
                 e.tensor_mul(out=o, in0=i0, in1=i1))
        for et in range(4):
            dl = delta_r[:, et * KC:(et + 1) * KC]
            for s in range(NS):
                k = (ch * 4 + et) * NS + s
                da = dAb[k % 2]
                if s == 0:
                    b.waitr("scalar", sDelC, 4 * ch + et + 1)
                if k >= 2:
                    b.waitr("scalar", sSCN, k - 1)
                sDA.n += 1
                b.on("scalar", lambda e, o=da, i=dl, sc=A4[:, (et * NS + s):(et * NS + s) + 1], hh=sDA.h:
                     e.activation(out=o, in_=i, func=AF.Exp, scale=sc).then_inc(hh, 1))
                dx = dxb[k % 2]
                b.on("vector", lambda e, o=dx, i0=u_r[:, et * KC:(et + 1) * KC], i1=Bb[:, s * KC:(s + 1) * KC]:
                     e.tensor_mul(out=o, in0=i0, in1=i1))
                b.wait("vector", sDA)
                sSCN.n += 1
                b.on("vector", lambda e, o=HS[:, s * KC:(s + 1) * KC], d0=da, d1=dx,
                     ini=st_carry[:, et * NS + s:et * NS + s + 1], hh=sSCN.h:
                     e.tensor_tensor_scan(out=o, data0=d0, data1=d1, initial=ini,
                                          op0=ALU.mult, op1=ALU.add).then_inc(hh, 1))
            b.wait("vector", sSCN)
            b.on("vector", lambda e, o=st_carry[:, et * NS:(et + 1) * NS],
                 i=HS[:, KC - 1:NS * KC:KC]:
                 e.tensor_copy(out=o, in_=i))
            b.on("vector", lambda e: e.tensor_mul(out=HS[:, :], in0=HS[:, :], in1=Cb[:, :]))
            b.on("vector", lambda e, o=yb, i=HS[:, :].rearrange("p (s t) -> p t s", s=NS):
                 e.tensor_reduce(out=o, in_=i, axis=AX.X, op=ALU.add))
            b.on("vector", lambda e, o=yb, i=XC[:, et * T + ch * KC:et * T + ch * KC + KC],
                 s=sb_dp[:, et:et + 1]:
                 e.scalar_tensor_tensor(out=o, in0=i, scalar=s, in1=o, op0=ALU.mult, op1=ALU.add))
            zbank = ps[2 + et % 2]
            g = 4 * ch + et
            if g >= 2:
                b.waitr("tensor", sSil3, g - 1)
            for mt in range(NDT):
                mm(b, zbank[:, :], winz_bf[:, mt * KC + et * 128:mt * KC + (et + 1) * 128],
                   xb[:, mt * KC:(mt + 1) * KC], mt == 0, mt == NDT - 1,
                   inc=(sZ.h, 1) if mt == NDT - 1 else None)
            sZ.n += 1
            szv = szb[g % 2]
            b.wait("scalar", sZ)
            if g >= 2:
                b.waitr("scalar", sYG, g - 1)
            sSil3.n += 1
            b.on("scalar", lambda e, o=szv, i=zbank[:, :], hh=sSil3.h:
                 e.activation(out=o, in_=i, func=AF.Silu).then_inc(hh, 1))
            yg = ygb[g % 2]
            b.wait("vector", sSil3)
            if g >= 2:
                b.wait("vector", sYGs[g % 2])
            sYG.n += 1
            sPRD.n += 1
            b.on("vector", lambda e, o=yg, i0=yb, i1=szv, hh=sYG.h:
                 e.tensor_mul(out=o, in0=i0, in1=i1).then_inc(hh, 1))
            b.on("vector", lambda e, hh=sPRD.h: e.sem_inc(hh, 1))
            b.wait("gpsimd", sYG)
            b.dma("gpsimd", sYGs[g % 2],
                  ag_yg_in[(ch // 2) * EC + et * 128:(ch // 2) * EC + (et + 1) * 128,
                           (ch % 2) * KC:(ch % 2) * KC + KC], yg)

    b.wait("gpsimd", sYGs[0])
    b.wait("gpsimd", sYGs[1])
    b.collective("AllToAll", ALU.bypass, ag_yg_in[:, :], YGL[:, :], sA2Ayg)
    b.tap("dbc", DBC, 160, T, F32, gate_sem=sARdbc)
    b.tap("ygin", ag_yg_in, E4, TC, BF16, gate_sem=sYGs)
    b.tap("ygl", YGL, E4, TC, BF16, gate_sem=sA2Ayg)
    arena.release()
    arena.release()

    arena.mark()
    ygl_sb = arena.alloc(32 * TC, BF16)
    wob = [arena.alloc(32 * 128, BF16) for _ in range(2)]
    ln3 = LnCtx(b, C, "ln3")
    sPO3 = b.psem()
    sYgl = b.psem()
    sWbL = b.psem2()

    b.waitr("sync", sA2Ayg, 1)
    b.waitr("sync", sAGwo2, 1)
    for et in range(32):
        b.dma("sync", sYgl, ygl_sb[:, et * TC:(et + 1) * TC], YGL[et * 128:(et + 1) * 128, :])
    b.wait("tensor", sYgl)
    for tch in range(2):
        if tch >= 1:
            b.wait("tensor", ln3.sBcE)
        for dm in range(NDT):
            gi = tch * NDT + dm
            wb = wob[gi % 2]
            if gi >= 2:
                b.waitr("sync", sPO3, gi - 1)
            for et in range(32):
                b.dma("sync", sWbL[gi % 2], wb[:, et * 128:(et + 1) * 128],
                      WOUTT[(et * 16 + dm) * 128:(et * 16 + dm + 1) * 128, :])
            b.wait("tensor", sWbL[gi % 2])
            bank = ps[gi % 2]
            if gi >= 2:
                b.waitr("tensor", ln3.sAdd, gi - 1)
            for et in range(32):
                mm(b, bank[:, :], wb[:, et * 128:(et + 1) * 128],
                   ygl_sb[:, et * TC + tch * KC:et * TC + tch * KC + KC],
                   et == 0, et == 31, inc=(sPO3.h, 1) if et == 31 else None)
            sPO3.n += 1
            b.wait("vector", sPO3)
            ln3.add_tile(dm, bank[:, :], X2F, tch * KC)
        ln3.stats_norm_store([(X3F, tch * KC, F32), (X3B, tch * KC, BF16)])
    arena.release()
    C.update(X3F=X3F, X3B=X3B, sX3=ln3.sSt, sLn3Add=ln3.sAdd)
    b.tap("x3f", X3F, D, TC, F32, gate_sem=ln3.sSt[0])


def build_p4(b, C):
    b.pool_reset()
    """L1 fp4 FFN: dequant (d1) + AG, gate/up (silu*), down, RMS, +res, LN."""
    arena, ps = C["arena"], C["ps"]
    sb_eye, sb_consts = C["sb_eye"], C["sb_consts"]
    X3F, X3B = C["X3F"], C["X3B"]
    NT1 = I1 // 128
    GUPC = 2 * I1 // W // 128
    cHALF = sb_consts[:, 4:5]
    cM2 = sb_consts[:, 5:6]
    cM6 = sb_consts[:, 6:7]

    gup = b.din("gup", [GUPC * 128, 256], I32)
    gusT = b.din("gusT", [GUPC * 128, 16], F16)
    dnp = b.din("dnp", [256, I1 // 8], I32)
    dnsT = b.din("dnsT", [256, NT1], F16)

    ag_w1gu_in = b.dint("ag_w1gu_in", [GUPC * 16 * 128, 128], BF16)
    W1GUT = b.dint("W1GUT", [2 * NT1 * 16 * 128, 128], BF16)
    ag_w1d_in = b.dint("ag_w1d_in", [2 * NT1 * 128, 128], BF16)
    W1DT = b.dint("W1DT", [NDT * NT1 * 128, 128], BF16)
    OUT = b.dout("out", [D, TC], F32)
    sAGw1 = b.sem("sAGw1")

    arena.mark()
    pk = [arena.alloc(256, I32) for _ in range(2)]
    sc16 = [arena.alloc(16, F16) for _ in range(2)]
    scf = [arena.alloc(16, F32) for _ in range(2)]
    m2sc = [arena.alloc(16, F32) for _ in range(2)]
    nib = arena.alloc(2048, I32)
    fbv = arena.alloc(2048, F32)
    sfb = arena.alloc(2048, F32)
    l3b = arena.alloc(2048, F32)
    t1b = [arena.alloc(2048, F32) for _ in range(2)]
    t2b = [arena.alloc(2048, F32) for _ in range(2)]
    ub = arena.alloc(2048, F32)
    sgn = arena.alloc(2048, F32)
    wbf = [arena.alloc(2048, BF16) for _ in range(2)]
    wst = [arena.alloc(1024, BF16) for _ in range(2)]

    sPkL = b.psem2()
    sScL = b.psem2()
    sNib = b.psem()
    sL3 = b.psem()
    sT12 = b.psem()
    sUd = b.psem()
    sWbf = b.psem()
    sTrp = b.psem()
    sEvc = b.psem()
    sD1S = b.psem2()

    b.wait("tensor", C["sLn3Add"])
    unit = [0]
    bgrp = [0]

    def d1_unit(src_hbm, r0, c0i, nci, sc_hbm, sc_r0, sc_c0, ag_buf, row_of):
        k = unit[0]
        unit[0] += 1
        p = k % 2
        nv = nci * 8
        ng = nv // 128
        V = lambda f: b.on("vector", f)
        if k >= 2:
            b.waitr("sync", sNib, k - 1)
        b.dma("sync", sPkL[p], pk[p][:, 0:nci], src_hbm[r0:r0 + 128, c0i:c0i + nci])
        b.dma("sync", sScL[p], sc16[p][:, 0:ng], sc_hbm[sc_r0:sc_r0 + 128, sc_c0:sc_c0 + ng])
        b.wait("vector", sPkL[p])
        b.wait("vector", sScL[p])
        V(lambda e, o=scf[p][:, 0:ng], i=sc16[p][:, 0:ng]: e.tensor_copy(out=o, in_=i))
        for kk in range(8):
            ov = nib[:, kk:nv:8]
            iv = pk[p][:, 0:nci]
            if kk == 7:
                sNib.n += 1
                V(lambda e, o=ov, i=iv, s1=4 * kk, hh=sNib.h:
                  e.tensor_scalar(out=o, in0=i, scalar1=s1, scalar2=15,
                                  op0=ALU.logical_shift_right, op1=ALU.bitwise_and).then_inc(hh, 1))
            else:
                V(lambda e, o=ov, i=iv, s1=4 * kk:
                  e.tensor_scalar(out=o, in0=i, scalar1=s1, scalar2=15,
                                  op0=ALU.logical_shift_right, op1=ALU.bitwise_and))
        b.wait("vector", sNib)
        V(lambda e, o=fbv[:, 0:nv], i=nib[:, 0:nv]: e.tensor_copy(out=o, in_=i))
        V(lambda e, o=sfb[:, 0:nv], i=fbv[:, 0:nv]:
          e.tensor_single_scalar(out=o, in_=i, scalar=8.0, op=ALU.is_ge))
        sL3.n += 1
        V(lambda e, o=l3b[:, 0:nv], i0=sfb[:, 0:nv], i1=fbv[:, 0:nv], hh=sL3.h:
          e.scalar_tensor_tensor(out=o, in0=i0, scalar=-8.0, in1=i1,
                                 op0=ALU.mult, op1=ALU.add).then_inc(hh, 1))
        b.waitr("scalar", sL3, k + 1)
        if k >= 2:
            b.waitr("scalar", sUd, k - 1)
        sT12.n += 1
        b.on("scalar", lambda e, o=t1b[p][:, 0:nv], i=l3b[:, 0:nv]:
             e.activation(out=o, in_=i, func=AF.Relu, scale=cHALF, bias=cM2))
        b.on("scalar", lambda e, o=t2b[p][:, 0:nv], i=l3b[:, 0:nv], hh=sT12.h:
             e.activation(out=o, in_=i, func=AF.Relu, bias=cM6).then_inc(hh, 1))
        b.waitr("vector", sT12, k + 1)
        V(lambda e, o=ub[:, 0:nv], i0=l3b[:, 0:nv], i1=t1b[p][:, 0:nv]:
          e.scalar_tensor_tensor(out=o, in0=i0, scalar=0.5, in1=i1,
                                 op0=ALU.mult, op1=ALU.add))
        sUd.n += 1
        V(lambda e, o=ub[:, 0:nv], i0=ub[:, 0:nv], i1=t2b[p][:, 0:nv], hh=sUd.h:
          e.tensor_add(out=o, in0=i0, in1=i1).then_inc(hh, 1))
        V(lambda e, o=sgn[:, 0:nv], i=sfb[:, 0:nv]:
          e.tensor_scalar(out=o, in0=i, scalar1=-2.0, scalar2=1.0,
                          op0=ALU.mult, op1=ALU.add))
        for gg in range(ng):
            V(lambda e, o=sgn[:, gg * 128:(gg + 1) * 128],
              s=scf[p][:, gg:gg + 1]:
              e.tensor_single_scalar(out=o, in_=o, scalar=s, op=ALU.mult))
        if k >= 2:
            b.waitr("vector", sTrp, 2 * k - 2)
        sWbf.n += 1
        V(lambda e, o=wbf[p][:, 0:nv], i0=ub[:, 0:nv], i1=sgn[:, 0:nv], hh=sWbf.h:
          e.tensor_tensor(out=o, in0=i0, in1=i1, op=ALU.mult).then_inc(hh, 1))
        b.waitr("tensor", sWbf, k + 1)
        j0 = 0
        while j0 < ng:
            gcnt = min(8, ng - j0)
            g = bgrp[0]
            bgrp[0] += 1
            bank = ps[g % 4][:, :].bitcast(BF16)
            if g >= 4:
                b.waitr("tensor", sEvc, g - 3)
            for t in range(gcnt):
                fin = t == gcnt - 1
                if fin:
                    sTrp.n += 1
                def tr(e, o=bank[:, t * 128:(t + 1) * 128],
                       i=wbf[p][:, (j0 + t) * 128:(j0 + t + 1) * 128], f=fin, hh=sTrp.h):
                    ins = e.transpose(o, i, sb_eye)
                    if f:
                        ins.then_inc(hh, 1)
                b.on("tensor", tr)
            gp = g % 2
            b.waitr("vector", sTrp, g + 1)
            b.wait("vector", sD1S[gp])
            sEvc.n += 1
            V(lambda e, o=wst[gp][:, 0:gcnt * 128], i=bank[:, 0:gcnt * 128], hh=sEvc.h:
              e.tensor_copy(out=o, in_=i).then_inc(hh, 1))
            b.waitr("scalar", sEvc, g + 1)
            r0s = row_of(j0)
            for t in range(gcnt):
                b.dma("scalar", sD1S[gp],
                      ag_buf[r0s + t * 128:r0s + (t + 1) * 128, :],
                      wst[gp][:, t * 128:(t + 1) * 128])
            j0 += gcnt

    for rt in range(GUPC):
        d1_unit(gup, rt * 128, 0, 256, gusT, rt * 128, 0,
                ag_w1gu_in, lambda j0, rt=rt: (rt * 16 + j0) * 128)
    for gl in range(2):
        for cc, (c0i, nci) in enumerate(((0, 256), (256, 256), (512, 192))):
            d1_unit(dnp, gl * 128, c0i, nci, dnsT, gl * 128, cc * 16,
                    ag_w1d_in, lambda j0, gl=gl, cc=cc: (gl * NT1 + cc * 16 + j0) * 128)

    b.wait("gpsimd", sD1S[0])
    b.wait("gpsimd", sD1S[1])
    b.collective("AllGather", ALU.bypass, ag_w1gu_in[:, :], W1GUT[:, :], sAGw1)
    b.collective("AllGather", ALU.bypass, ag_w1d_in[:, :], W1DT[:, :], sAGw1)
    arena.release()
    b.tap("w1gut", W1GUT, 4096, 128, BF16, gate_sem=sAGw1)
    b.tap("w1dt", W1DT, 4096, 128, BF16, gate_sem=sAGw1)

    arena.mark()
    xb3_sb = arena.alloc(NDT * TC, BF16)
    H4 = arena.alloc(NT1 * TC, BF16)
    arena.mark()
    wbuf = [arena.alloc(32 * 128, BF16) for _ in range(2)]
    sg_buf = [arena.alloc(KC, BF16) for _ in range(2)]

    sP4g = b.psem()
    sSil = b.psem()
    sHE = b.psem()
    sDnG = b.psem()
    sX3bL = b.psem()
    sW1L = b.psem2()
    sWcL = b.psem2()

    b.wait("sync", sAGw1, 2)
    for mt in range(NDT):
        b.dma("sync", sX3bL, xb3_sb[:, mt * TC:(mt + 1) * TC], X3B[mt * 128:(mt + 1) * 128, :])
    b.wait("tensor", sX3bL)
    for it in range(NT1):
        wb = wbuf[it % 2]
        if it >= 2:
            b.waitr("sync", sP4g, 4 * (it - 1))
        for mt in range(NDT):
            row = (it * 16 + mt) * 128
            b.dma("sync", sW1L[it % 2], wb[:, mt * 128:(mt + 1) * 128], W1GUT[row:row + 128, :])
        for mt in range(NDT):
            row = ((NT1 + it) * 16 + mt) * 128
            b.dma("sync", sW1L[it % 2], wb[:, (16 + mt) * 128:(17 + mt) * 128], W1GUT[row:row + 128, :])
        b.wait("tensor", sW1L[it % 2])
        for tch in range(2):
            for gu in range(2):
                bank = ps[(it % 2) * 4 + tch * 2 + gu]
                if it >= 2:
                    if gu == 0:
                        b.waitr("tensor", sSil, 2 * (it - 1))
                    else:
                        b.waitr("tensor", sHE, 2 * (it - 1))
                for mt in range(NDT):
                    mm(b, bank[:, :], wb[:, (gu * 16 + mt) * 128:(gu * 16 + mt + 1) * 128],
                       xb3_sb[:, mt * TC + tch * KC:mt * TC + tch * KC + KC],
                       mt == 0, mt == NDT - 1, inc=(sP4g.h, 1) if mt == NDT - 1 else None)
                sP4g.n += 1
        for tch in range(2):
            gbank = ps[(it % 2) * 4 + tch * 2 + 0]
            ubank = ps[(it % 2) * 4 + tch * 2 + 1]
            sgb = sg_buf[tch]
            kk = 2 * it + tch
            if kk >= 2:
                b.waitr("scalar", sHE, kk - 1)
            b.waitr("scalar", sP4g, 4 * it + tch * 2 + 1)
            sSil.n += 1
            b.on("scalar", lambda e, o=sgb, i=gbank[:, :], hh=sSil.h:
                 e.activation(out=o, in_=i, func=AF.Silu).then_inc(hh, 1))
            b.wait("vector", sSil)
            b.waitr("vector", sP4g, 4 * it + tch * 2 + 2)
            sHE.n += 1
            b.on("vector", lambda e, o=H4[:, it * TC + tch * KC:it * TC + tch * KC + KC],
                 i0=sgb, i1=ubank[:, :], hh=sHE.h:
                 e.tensor_tensor(out=o, in0=i0, in1=i1, op=ALU.mult).then_inc(hh, 1))

    arena.release()
    wcol = [arena.alloc(NT1 * 128, BF16) for _ in range(2)]
    ln4 = LnCtx(b, C, "ln4")
    b.wait("tensor", sHE)
    b.wait("sync", sP4g)
    b.wait("sync", sHE)
    b.wait("scalar", sP4g)
    b.wait("scalar", sHE)
    for tch in range(2):
        if tch >= 1:
            b.wait("tensor", ln4.sBcE)
        for g in range(2):
            if tch + g > 0:
                b.wait("tensor", ln4.sAdd)
            for dml in range(8):
                dm = g * 8 + dml
                gidx = (tch * 2 + g) * 8 + dml
                wc = wcol[gidx % 2]
                q = "sync" if gidx % 2 == 0 else "scalar"
                qs = sWcL[gidx % 2]
                if gidx >= 2:
                    b.waitr(q, sDnG, gidx - 1)
                for i in range(NT1):
                    b.dma(q, qs, wc[:, i * 128:(i + 1) * 128],
                          W1DT[(dm * NT1 + i) * 128:(dm * NT1 + i + 1) * 128, :])
                b.wait("tensor", qs)
                for i in range(NT1):
                    mm(b, ps[dml][:, :], wc[:, i * 128:(i + 1) * 128],
                       H4[:, i * TC + tch * KC:i * TC + tch * KC + KC],
                       i == 0, i == NT1 - 1, inc=(sDnG.h, 1) if i == NT1 - 1 else None)
                sDnG.n += 1
            b.wait("vector", sDnG)
            for dml in range(8):
                dm = g * 8 + dml
                ln4.sAdd.n += 1
                b.on("vector", lambda e, o=ln4.xpre[:, dm * KC:(dm + 1) * KC],
                     i=ps[dml][:, :], hh=ln4.sAdd.h:
                     e.tensor_copy(out=o, in_=i).then_inc(hh, 1))
        ln4.rms_scale_inplace()
        for dm in range(NDT):
            ln4.add_tile(dm, ln4.xpre[:, dm * KC:(dm + 1) * KC], X3F, tch * KC)
        ln4.stats_norm_store([(OUT, tch * KC, F32)])
    arena.release()


def make_mask_t():
    k = np.arange(KC)[:, None]
    out = np.zeros((KC, 4 * 128), dtype=np.float32)
    for v in range(4):
        q = np.arange(128)[None, :]
        out[:, v * 128:(v + 1) * 128] = np.where(k <= v * 128 + q, 0.0, np.float32(MASKV))
    return out.astype(bfdt)


def host_inputs(inputs, c, phases):
    x = np.asarray(inputs["L0_attn_hidden_states"])[0]
    d = {
        "xtf": np.ascontiguousarray(x[c * TC:(c + 1) * TC, :].T),
        "onesb": np.ones((128, 1), dtype=bfdt),
        "eyeb": np.eye(128, dtype=bfdt),
        "ones1f": np.ones((1, 128), dtype=np.float32),
        "consts": np.tile(np.array([[EPS_LN, EPS_RMS, -0.5, QSC, 0.5, -2.0, -6.0, 1.0]],
                                   dtype=np.float32), (128, 1)),
    }
    if "p1" in phases:
        d.update({
            "wq": np.ascontiguousarray(np.asarray(inputs["L0_attn_q_wei"])[:, c * 256:(c + 1) * 256]),
            "wk": np.ascontiguousarray(np.asarray(inputs["L0_attn_k_wei"])[:, c * 256:(c + 1) * 256]),
            "wv": np.ascontiguousarray(np.asarray(inputs["L0_attn_v_wei"])[:, c * 256:(c + 1) * 256]),
            "wo": np.asarray(inputs["L0_attn_out_wei"]),
            "maskt": make_mask_t(),
        })
    if "p2" in phases:
        d.update({
            "g_idxT": np.ascontiguousarray(np.asarray(inputs["L0_ffn_gate_idx"])[:, c * 256:(c + 1) * 256].T),
            "u_idxT": np.ascontiguousarray(np.asarray(inputs["L0_ffn_up_idx"])[:, c * 256:(c + 1) * 256].T),
            "d_idxT": np.ascontiguousarray(np.asarray(inputs["L0_ffn_down_idx"])[:, c * 688:(c + 1) * 688].T),
            "lutb": np.tile(np.asarray(inputs["L0_ffn_lut"])[None, :].astype(np.float32), (128, 1)),
        })
    if "p3" in phases:
        W_in = np.asarray(inputs["L1_attn_W_in"])
        sl = slice(c * EC, (c + 1) * EC)

        def v4(a):
            return np.ascontiguousarray(a[sl].reshape(4, 128).T)
        d.update({
            "winxT": np.ascontiguousarray(W_in[sl, :].T),
            "winzT": np.ascontiguousarray(W_in[E4 + c * EC:E4 + (c + 1) * EC, :].T),
            "convw4": np.ascontiguousarray(
                np.asarray(inputs["L1_attn_conv_w"])[sl].reshape(4, 128, 4).transpose(1, 0, 2).reshape(128, 16)),
            "convb4": v4(np.asarray(inputs["L1_attn_conv_b"])),
            "wxT": np.ascontiguousarray(np.asarray(inputs["L1_attn_W_x"])[:, sl].T),
            "wdtT": np.ascontiguousarray(np.asarray(inputs["L1_attn_W_dt"])[sl, :].T),
            "bdt4": v4(np.asarray(inputs["L1_attn_b_dt"])),
            "alog4": np.ascontiguousarray(
                np.asarray(inputs["L1_attn_A_log"])[sl].reshape(4, 128, NS).transpose(1, 0, 2).reshape(128, 4 * NS)),
            "dp4": v4(np.asarray(inputs["L1_attn_Dp"])),
            "woutT_t": np.ascontiguousarray(
                np.asarray(inputs["L1_attn_W_out"])[:, sl].T.reshape(4, 128, 16, 128)
                .transpose(0, 2, 1, 3).reshape(EC * 16, 128)),
        })
    if "p4" in phases:
        gupr = 2 * I1 // W
        d.update({
            "gup": np.ascontiguousarray(
                np.asarray(inputs["L1_ffn_gate_up_packed"])[c * gupr:(c + 1) * gupr, :]),
            "gusT": np.ascontiguousarray(
                np.asarray(inputs["L1_ffn_gate_up_scales"])[:, c * gupr:(c + 1) * gupr].T),
            "dnp": np.ascontiguousarray(
                np.asarray(inputs["L1_ffn_down_packed"])[c * 256:(c + 1) * 256, :]),
            "dnsT": np.ascontiguousarray(
                np.asarray(inputs["L1_ffn_down_scales"])[:, c * 256:(c + 1) * 256].T),
        })
    return d


ALL_PHASES = ("p1", "p2", "p3", "p4")


def kernel(**inputs):
    bb = build(phases=ALL_PHASES, taps=())
    in_maps = [host_inputs(inputs, c, set(ALL_PHASES)) for c in range(W)]
    res = run_bass_kernel_spmd(bb.nc, in_maps, core_ids=list(range(W)))
    parts = [np.asarray(res.results[c]["out"]).T for c in range(W)]
    return np.ascontiguousarray(
        np.concatenate(parts, axis=0).reshape(1, T, D)).astype(np.float32, copy=False)


def run_phases(inputs, phases=("p1",), taps=()):
    bb = build(phases=phases, taps=taps)
    in_maps = [host_inputs(inputs, c, set(phases)) for c in range(W)]
    res = run_bass_kernel_spmd(bb.nc, in_maps, core_ids=list(range(W)))
    return res.results
